# version 02
# baseline (speedup 1.0000x reference)
import numpy as np
import ml_dtypes
from contextlib import ExitStack

import concourse.bass as bass
import concourse.mybir as mybir
from concourse.bass import ts, OrderedEngineSet
from concourse.bass_utils import run_bass_kernel_spmd

F32 = mybir.dt.float32
BF16 = mybir.dt.bfloat16
F16 = mybir.dt.float16
I32 = mybir.dt.int32
AF = mybir.ActivationFunctionType
OP = mybir.AluOpType
AX = mybir.AxisListType.X
BF = ml_dtypes.bfloat16

NCORES = 8
DETECT_RACE = False
D = 2048
KC = D // 128
TPC = 2048
TT = 512
NTILES = TPC // TT
HID = 8192
NHEAD, HDIM, CHUNK = 16, 128, 256
ENGINES = ("sync", "vector", "scalar", "tensor", "gpsimd")


class _EngRec:

    def __init__(self, eng, ctx):
        self._eng = eng
        self._ctx = ctx

    def __getattr__(self, name):
        fn = getattr(self._eng, name)
        if not callable(fn):
            return fn
        ctx = self._ctx

        def wrap(*a, **kw):
            r = fn(*a, **kw)
            ctx.last = r
            return r
        return wrap


class _Ctx:
    def __init__(self, eng, dsem):
        self.raw = eng
        self.e = _EngRec(eng, self)
        self.dsem = dsem
        self.ndma = 0
        self.last = None

    def dma(self, out, in_, slow=False):
        self.raw.dma_start(out=out, in_=in_,
                           allow_slow_non_contiguous=slow).then_inc(
            self.dsem, 16)
        self.ndma += 1


class SP:

    def __init__(self, nc):
        self.nc = nc
        self.stages = []

    def stage(self, name=""):
        st = {e: [] for e in ENGINES}
        self.stages.append(st)

        class Rec:
            pass

        r = Rec()
        for e in ENGINES:
            def add(fn, e=e, st=st):
                st[e].append(fn)
                return fn
            setattr(r, e, add)
        return r

    def emit(self, ctx):
        nc = self.nc
        stg = ctx.enter_context(nc.semaphore("stg"))
        dsems = {e: ctx.enter_context(nc.semaphore(f"dsem_{e}"))
                 for e in ENGINES}
        parts = [[e for e in ENGINES if st[e]] for st in self.stages]
        thr = []
        tot = 0
        for p in parts:
            tot += len(p)
            thr.append(tot)
        block = ctx.enter_context(nc.Block())
        for ename in ENGINES:
            def body(eng, ename=ename):
                c = _Ctx(eng, dsems[ename])
                waited = 0
                for si, st in enumerate(self.stages):
                    fns = st[ename]
                    if not fns:
                        continue
                    need = thr[si - 1] if si > 0 else 0
                    if need > 0:
                        eng.wait_ge(stg, need)
                    c.last = None
                    for fn in fns:
                        fn(c)
                    if c.ndma > waited:
                        eng.wait_ge(dsems[ename], 16 * c.ndma)
                        waited = c.ndma
                        eng.nop().then_inc(stg, 1)
                    elif c.last is not None:
                        eng.drain().then_inc(stg, 1)
                    else:
                        eng.nop().then_inc(stg, 1)
            getattr(block, ename)(body)


def alloc_common(nc, ctx):
    t = {}
    t["ones_col_b"] = ctx.enter_context(
        nc.sbuf_tensor("ones_col_b", [128, 1], BF16))
    t["ones_row_f"] = ctx.enter_context(
        nc.sbuf_tensor("ones_row_f", [1, 128], F32))
    return t


def alloc_ln(nc, ctx, t, pref, n):
    sb = lambda nm, sh, dt: ctx.enter_context(nc.sbuf_tensor(nm, sh, dt))
    t[pref + "_xb"] = sb(pref + "_xb", [128, KC * n], BF16)
    t[pref + "_xsq"] = sb(pref + "_xsq", [128, KC * n], BF16)
    t[pref + "_ps_s"] = ctx.enter_context(
        nc.psum_tensor(pref + "_ps_s", [1, n], F32))
    t[pref + "_ps_q"] = ctx.enter_context(
        nc.psum_tensor(pref + "_ps_q", [1, n], F32))
    for nm in ("rm", "rq", "rt", "rr"):
        t[pref + "_" + nm] = sb(pref + "_" + nm, [1, n], F32)
    t[pref + "_mb"] = sb(pref + "_mb", [128, n], F32)
    t[pref + "_rb"] = sb(pref + "_rb", [128, n], F32)
    t[pref + "_ps_bm"] = ctx.enter_context(
        nc.psum_tensor(pref + "_ps_bm", [128, n], F32))
    t[pref + "_ps_br"] = ctx.enter_context(
        nc.psum_tensor(pref + "_ps_br", [128, n], F32))


def emit_ln_stats(sp, nc, t, xin_chunks, n, pref):
    xb, xsq = t[pref + "_xb"], t[pref + "_xsq"]
    ps_s, ps_q = t[pref + "_ps_s"], t[pref + "_ps_q"]
    rm, rq, rt, rr = (t[pref + "_rm"], t[pref + "_rq"],
                      t[pref + "_rt"], t[pref + "_rr"])
    s = sp.stage(pref + "_cast")
    for i in range(KC):
        s.vector(lambda c, i=i: c.e.tensor_copy(
            xb[:, i * n:(i + 1) * n], xin_chunks[i]))
        s.scalar(lambda c, i=i: c.e.activation(
            xsq[:, i * n:(i + 1) * n], xin_chunks[i], AF.Square))
    s = sp.stage(pref + "_summ")
    def mm_sums(c):
        for i in range(KC):
            c.e.matmul(ps_s[:], t["ones_col_b"][:], xb[:, i * n:(i + 1) * n],
                       start=(i == 0), stop=(i == KC - 1))
        for i in range(KC):
            c.e.matmul(ps_q[:], t["ones_col_b"][:], xsq[:, i * n:(i + 1) * n],
                       start=(i == 0), stop=(i == KC - 1))
    s.tensor(mm_sums)
    s = sp.stage(pref + "_rows")
    def rows(c):
        v = c.e
        v.tensor_scalar_mul(rm[:], ps_s[:], 1.0 / D)
        v.tensor_scalar_mul(rq[:], ps_q[:], 1.0 / D)
        v.drain()
        v.tensor_mul(rt[:], rm[:], rm[:])
        v.drain()
        v.tensor_sub(rq[:], rq[:], rt[:])
        v.drain()
        v.tensor_scalar_add(rq[:], rq[:], 1e-5)
    s.vector(rows)
    s = sp.stage(pref + "_rstd")
    def rstd(c):
        sc = c.e
        sc.activation(rt[:], rq[:], AF.Ln)
        sc.drain()
        sc.activation(rr[:], rt[:], AF.Exp, scale=-0.5)
    s.scalar(rstd)
    s = sp.stage(pref + "_bcast")
    def bc(c):
        c.e.matmul(t[pref + "_ps_bm"][:], t["ones_row_f"][:], rm[:],
                   start=True, stop=True)
        c.e.matmul(t[pref + "_ps_br"][:], t["ones_row_f"][:], rr[:],
                   start=True, stop=True)
    s.tensor(bc)
    s = sp.stage(pref + "_bev")
    s.vector(lambda c: c.e.tensor_copy(t[pref + "_mb"][:],
                                       t[pref + "_ps_bm"][:]))
    s.vector(lambda c: c.e.tensor_copy(t[pref + "_rb"][:],
                                       t[pref + "_ps_br"][:]))


def emit_ln_apply(sp, name, xin, out, mb, rb, scr4, n):
    s = sp.stage(name)
    def go(c):
        v = c.e
        for r in range(4):
            for j in range(4):
                i = r * 4 + j
                v.tensor_sub(scr4[:, j * n:(j + 1) * n], xin[i], mb[:])
            v.drain()
            for j in range(4):
                i = r * 4 + j
                v.tensor_mul(out[i], scr4[:, j * n:(j + 1) * n], rb[:])
            v.drain()
    s.vector(go)


def emit_res_add(sp, name, x_chunks, ln_chunks):
    s = sp.stage(name)
    def go(c):
        v = c.e
        for i in range(KC):
            v.tensor_add(x_chunks[i], x_chunks[i], ln_chunks[i])
    s.vector(go)


def phase_a(nc, sp, t, io):
    xT, prev, wk, wv, wr, tm = (io["xT"], io["prev"], io["wkT"], io["wvT"],
                                io["wrT"], io["tm"])
    ko, vo, so = io["k_d"], io["v_d"], io["sr_d"]
    with ExitStack() as ctx:
        sb = lambda nm, sh, dt=F32: ctx.enter_context(nc.sbuf_tensor(nm, sh, dt))
        psf = lambda nm, sh: ctx.enter_context(nc.psum_tensor(nm, sh, F32))
        xt = sb("a_xt", [128, KC * TT])
        alloc_ln(nc, ctx, t, "aln", TT)
        xln = sb("a_xln", [128, KC * TT], BF16)
        lxcol = sb("a_lxcol", [128, KC], BF16)
        prevt = sb("a_prevt", [128, KC])
        prevb = sb("a_prevb", [128, KC], BF16)
        prevq = sb("a_prevq", [128, KC], BF16)
        tmt = sb("a_tmt", [128, KC * 3])
        dmx = sb("a_dmx", [128, 4 * TT], BF16)
        mix = sb("a_mix", [128, 3 * KC * TT], BF16)
        wb = sb("a_wb", [128, KC * 512], BF16)
        evac = sb("a_evac", [128, 4 * TT])
        ps_o = [psf(f"a_ps_o{i}", [128, TT]) for i in range(4)]

        s = sp.stage("tm_load")
        s.sync(lambda c: c.dma(
            tmt[:].rearrange("p (kc three) -> p kc three", three=3),
            tm.rearrange("(kc p) three -> p kc three", p=128), slow=True))
        s.sync(lambda c: c.dma(
            prevt[:].rearrange("p (kc one) -> p kc one", one=1),
            prev.rearrange("(kc p) one -> p kc one", p=128), slow=True))
        s = sp.stage("pv_cast")
        s.vector(lambda c: c.e.tensor_copy(prevb[:], prevt[:]))
        s.scalar(lambda c: c.e.activation(prevq[:], prevt[:], AF.Square))
        s = sp.stage("pv_summ")
        def pv_mm(c):
            c.e.matmul(t["aln_ps_s"][:, 0:KC], t["ones_col_b"][:], prevb[:],
                       start=True, stop=True)
            c.e.matmul(t["aln_ps_q"][:, 0:KC], t["ones_col_b"][:], prevq[:],
                       start=True, stop=True)
        s.tensor(pv_mm)
        s = sp.stage("pv_rows")
        def pv_rows(c):
            v = c.e
            v.reduce_sum(t["aln_rm"][:, 0:1], t["aln_ps_s"][:, 0:KC], axis=AX)
            v.reduce_sum(t["aln_rq"][:, 0:1], t["aln_ps_q"][:, 0:KC], axis=AX)
            v.drain()
            v.tensor_scalar_mul(t["aln_rm"][:, 0:1], t["aln_rm"][:, 0:1], 1.0 / D)
            v.tensor_scalar_mul(t["aln_rq"][:, 0:1], t["aln_rq"][:, 0:1], 1.0 / D)
            v.drain()
            v.tensor_mul(t["aln_rt"][:, 0:1], t["aln_rm"][:, 0:1],
                         t["aln_rm"][:, 0:1])
            v.drain()
            v.tensor_sub(t["aln_rq"][:, 0:1], t["aln_rq"][:, 0:1],
                         t["aln_rt"][:, 0:1])
            v.drain()
            v.tensor_scalar_add(t["aln_rq"][:, 0:1], t["aln_rq"][:, 0:1], 1e-5)
        s.vector(pv_rows)
        s = sp.stage("pv_rstd")
        def pv_rstd(c):
            sc = c.e
            sc.activation(t["aln_rt"][:, 0:1], t["aln_rq"][:, 0:1], AF.Ln)
            sc.drain()
            sc.activation(t["aln_rr"][:, 0:1], t["aln_rt"][:, 0:1], AF.Exp,
                          scale=-0.5)
        s.scalar(pv_rstd)
        s = sp.stage("pv_bc")
        def pv_bc(c):
            c.e.matmul(t["aln_ps_bm"][:, 0:1], t["ones_row_f"][:],
                       t["aln_rm"][:, 0:1], start=True, stop=True)
            c.e.matmul(t["aln_ps_br"][:, 0:1], t["ones_row_f"][:],
                       t["aln_rr"][:, 0:1], start=True, stop=True)
        s.tensor(pv_bc)
        s = sp.stage("pv_apply")
        def pv_ap(c):
            v = c.e
            v.tensor_copy(t["aln_mb"][:, 0:1], t["aln_ps_bm"][:, 0:1])
            v.tensor_copy(t["aln_rb"][:, 0:1], t["aln_ps_br"][:, 0:1])
            v.drain()
            v.tensor_scalar(lxcol[:], prevt[:], t["aln_mb"][:, 0:1],
                            t["aln_rb"][:, 0:1], OP.subtract, OP.mult)
        s.vector(pv_ap)

        for tile in range(NTILES):
            off = tile * TT
            s = sp.stage(f"x_load{tile}")
            s.sync(lambda c, off=off: c.dma(
                xt[:].rearrange("p (kc n) -> p kc n", kc=KC),
                xT[:, off:off + TT].rearrange("(kc p) n -> p kc n", p=128)))
            xin = [xt[:, i * TT:(i + 1) * TT] for i in range(KC)]
            emit_ln_stats(sp, nc, t, xin, TT, "aln")
            emit_ln_apply(sp, f"a_xln{tile}", xin,
                          [xln[:, i * TT:(i + 1) * TT] for i in range(KC)],
                          t["aln_mb"], t["aln_rb"], evac, TT)
            s = sp.stage(f"a_mix{tile}")
            def mx_all(c):
                v = c.e
                for r in range(4):
                    for j in range(4):
                        i = r * 4 + j
                        base = i * TT
                        v.tensor_sub(dmx[:, j * TT + 1:(j + 1) * TT],
                                     xln[:, base + 1:base + TT],
                                     xln[:, base:base + TT - 1])
                        v.tensor_sub(dmx[:, j * TT:j * TT + 1],
                                     xln[:, base:base + 1], lxcol[:, i:i + 1])
                    v.drain()
                    for j in range(4):
                        i = r * 4 + j
                        base = i * TT
                        lx1 = xln[:, base:base + TT - 1]
                        lx0 = lxcol[:, i:i + 1]
                        for jj in range(3):
                            mslot = mix[:, (jj * KC + i) * TT:
                                        (jj * KC + i + 1) * TT]
                            v.scalar_tensor_tensor(
                                mslot[:, 1:TT], dmx[:, j * TT + 1:(j + 1) * TT],
                                tmt[:, i * 3 + jj:i * 3 + jj + 1], lx1,
                                OP.mult, OP.add)
                            v.scalar_tensor_tensor(
                                mslot[:, 0:1], dmx[:, j * TT:j * TT + 1],
                                tmt[:, i * 3 + jj:i * 3 + jj + 1], lx0,
                                OP.mult, OP.add)
                    v.drain()
            s.vector(mx_all)
            s = sp.stage(f"a_lxsave{tile}")
            def lxs(c):
                v = c.e
                for i in range(KC):
                    v.tensor_copy(lxcol[:, i:i + 1],
                                  xln[:, i * TT + TT - 1:i * TT + TT])
            s.vector(lxs)
            for pj, (wd, outd, isr) in enumerate(
                    [(wk, ko, False), (wv, vo, False), (wr, so, True)]):
                for ob in range(4):
                    s = sp.stage(f"a_w{tile}_{pj}_{ob}")
                    s.gpsimd(lambda c, wd=wd, ob=ob: c.dma(
                        wb[:].rearrange("p (kc o) -> p kc o", o=512),
                        wd[:, ob * 512:(ob + 1) * 512]
                        .rearrange("(kc p) o -> p kc o", p=128)))
                    s = sp.stage(f"a_mm{tile}_{pj}_{ob}")
                    def mms(c, pj=pj):
                        for oc in range(4):
                            for k in range(KC):
                                c.e.matmul(
                                    ps_o[oc][:],
                                    wb[:, k * 512 + oc * 128:
                                        k * 512 + (oc + 1) * 128],
                                    mix[:, (pj * KC + k) * TT:
                                        (pj * KC + k + 1) * TT],
                                    start=(k == 0), stop=(k == KC - 1))
                    s.tensor(mms)
                    s = sp.stage(f"a_ev{tile}_{pj}_{ob}")
                    for oc in range(4):
                        if isr:
                            s.scalar(lambda c, oc=oc: c.e.activation(
                                evac[:, oc * TT:(oc + 1) * TT], ps_o[oc][:],
                                AF.Sigmoid))
                        else:
                            s.vector(lambda c, oc=oc: c.e.tensor_copy(
                                evac[:, oc * TT:(oc + 1) * TT], ps_o[oc][:]))
                    s = sp.stage(f"a_st{tile}_{pj}_{ob}")
                    s.sync(lambda c, outd=outd, ob=ob, off=off: c.dma(
                        outd[ob * 512:(ob + 1) * 512, off:off + TT]
                        .rearrange("(oc p) n -> p oc n", p=128),
                        evac[:].rearrange("p (oc n) -> p oc n", oc=4)))


def phase_scan(nc, sp, t, io, tag, full, st):
    G = 8
    kd, vd = io["k_d"], io["v_d"]
    td, tf = io["td"], io["tf"]
    wo = io["wkv_d"]
    with ExitStack() as ctx:
        sb = lambda nm, sh, dt=F32: ctx.enter_context(nc.sbuf_tensor(nm, sh, dt))
        kt = sb(tag + "_kt", [128, G * TT])
        vt = sb(tag + "_vt", [128, G * TT])
        ekm = sb(tag + "_ekm", [128, G * TT])
        pt = sb(tag + "_pt", [128, G * TT])
        abuf = sb(tag + "_abuf", [128, G * (TT + 1)])
        bbuf = sb(tag + "_bbuf", [128, G * (TT + 1)])
        dect = sb(tag + "_dect", [128, G * TT])
        onesf = sb(tag + "_onesf", [128, TT])
        cols = sb(tag + "_cols", [128, 8 * G])
        Mprev = sb(tag + "_Mprev", [128, G])
        einit = sb(tag + "_einit", [128, G])
        if full:
            numt = sb(tag + "_numt", [128, G * TT])
            dent = sb(tag + "_dent", [128, G * TT])
            wkvt = sb(tag + "_wkvt", [128, G * TT])

        for ch in range(2):
            r0 = ch * 1024
            s = sp.stage(f"{tag}_cols{ch}")
            s.sync(lambda c, r0=r0: c.dma(
                cols[:, 0:G].rearrange("p (g one) -> p g one", one=1),
                td[r0:r0 + 1024].rearrange("(g p) one -> p g one", p=128),
                slow=True))
            s.sync(lambda c, r0=r0: c.dma(
                cols[:, G:2 * G].rearrange("p (g one) -> p g one", one=1),
                tf[r0:r0 + 1024].rearrange("(g p) one -> p g one", p=128),
                slow=True))
            s = sp.stage(f"{tag}_prep{ch}")
            def prep_s(c):
                sc = c.e
                sc.activation(cols[:, 2 * G:3 * G], cols[:, 0:G], AF.Exp)
                sc.activation(cols[:, 4 * G:5 * G], cols[:, G:2 * G], AF.Exp)
            s.scalar(prep_s)
            s = sp.stage(f"{tag}_prep2{ch}")
            s.scalar(lambda c: c.e.activation(cols[:, 3 * G:4 * G],
                                              cols[:, 2 * G:3 * G], AF.Exp,
                                              scale=-1.0))
            s = sp.stage(f"{tag}_prep3{ch}")
            def prep_v(c):
                v = c.e
                v.memset(onesf[:], 1.0)
                v.drain()
                for g in range(G):
                    v.tensor_scalar_mul(dect[:, g * TT:(g + 1) * TT], onesf[:],
                                        cols[:, 3 * G + g:3 * G + g + 1])
            s.vector(prep_v)

            for ti in range(NTILES):
                off = ti * TT
                s = sp.stage(f"{tag}_ld{ch}_{ti}")
                s.sync(lambda c, r0=r0, off=off: c.dma(
                    kt[:].rearrange("p (g n) -> p g n", g=G),
                    kd[r0:r0 + 1024, off:off + TT]
                    .rearrange("(g p) n -> p g n", p=128)))
                s.sync(lambda c, r0=r0, off=off: c.dma(
                    vt[:].rearrange("p (g n) -> p g n", g=G),
                    vd[r0:r0 + 1024, off:off + TT]
                    .rearrange("(g p) n -> p g n", p=128)))
                s = sp.stage(f"{tag}_max{ch}_{ti}")
                def mx(c, ti=ti, ch=ch):
                    v = c.e
                    v.reduce_max(cols[:, 5 * G:6 * G],
                                 kt[:].rearrange("p (g n) -> p g n", g=G),
                                 axis=AX)
                    v.drain()
                    v.tensor_scalar_mul(cols[:, 6 * G:7 * G],
                                        cols[:, 5 * G:6 * G], -1.0)
                    if ti > 0:
                        v.tensor_sub(cols[:, 7 * G:8 * G], Mprev[:],
                                     cols[:, 5 * G:6 * G])
                        v.drain()
                    elif full:
                        v.tensor_sub(einit[:], st["stE"][:, ch * G:(ch + 1) * G],
                                     cols[:, 5 * G:6 * G])
                        v.drain()
                    v.tensor_copy(Mprev[:], cols[:, 5 * G:6 * G])
                s.vector(mx)
                s = sp.stage(f"{tag}_exp{ch}_{ti}")
                def ex(c, ti=ti):
                    sc = c.e
                    for g in range(G):
                        sc.activation(ekm[:, g * TT:(g + 1) * TT],
                                      kt[:, g * TT:(g + 1) * TT], AF.Exp,
                                      bias=cols[:, 6 * G + g:6 * G + g + 1])
                    if ti > 0:
                        sc.activation(cols[:, 7 * G:8 * G], cols[:, 7 * G:8 * G],
                                      AF.Exp)
                    elif full:
                        sc.activation(einit[:], einit[:], AF.Exp)
                s.scalar(ex)
                s = sp.stage(f"{tag}_scan{ch}_{ti}")
                def scan(c, ti=ti, ch=ch):
                    v = c.e
                    v.tensor_mul(pt[:], ekm[:], vt[:])
                    for g in range(G):
                        a0 = abuf[:, g * (TT + 1):g * (TT + 1) + 1]
                        b0 = bbuf[:, g * (TT + 1):g * (TT + 1) + 1]
                        if ti == 0:
                            if full:
                                cg = ch * G + g
                                v.tensor_mul(a0, st["stAm"][:, cg:cg + 1],
                                             einit[:, g:g + 1])
                                v.tensor_mul(b0, st["stBm"][:, cg:cg + 1],
                                             einit[:, g:g + 1])
                            else:
                                v.memset(a0, 0.0)
                                v.memset(b0, 0.0)
                        else:
                            v.tensor_mul(a0, abuf[:, g * (TT + 1) + TT:
                                                  g * (TT + 1) + TT + 1],
                                         cols[:, 7 * G + g:7 * G + g + 1])
                            v.tensor_mul(b0, bbuf[:, g * (TT + 1) + TT:
                                                  g * (TT + 1) + TT + 1],
                                         cols[:, 7 * G + g:7 * G + g + 1])
                    v.drain()
                    for g in range(G):
                        sl = slice(g * (TT + 1) + 1, g * (TT + 1) + TT + 1)
                        v.tensor_tensor_scan(
                            abuf[:, sl], dect[:, g * TT:(g + 1) * TT],
                            pt[:, g * TT:(g + 1) * TT],
                            abuf[:, g * (TT + 1):g * (TT + 1) + 1],
                            OP.mult, OP.add)
                        v.tensor_tensor_scan(
                            bbuf[:, sl], dect[:, g * TT:(g + 1) * TT],
                            ekm[:, g * TT:(g + 1) * TT],
                            bbuf[:, g * (TT + 1):g * (TT + 1) + 1],
                            OP.mult, OP.add)
                    v.drain()
                    if full:
                        for g in range(G):
                            pv = slice(g * (TT + 1), g * (TT + 1) + TT)
                            v.scalar_tensor_tensor(
                                numt[:, g * TT:(g + 1) * TT],
                                pt[:, g * TT:(g + 1) * TT],
                                cols[:, 4 * G + g:4 * G + g + 1], abuf[:, pv],
                                OP.mult, OP.add)
                            v.scalar_tensor_tensor(
                                dent[:, g * TT:(g + 1) * TT],
                                ekm[:, g * TT:(g + 1) * TT],
                                cols[:, 4 * G + g:4 * G + g + 1], bbuf[:, pv],
                                OP.mult, OP.add)
                s.vector(scan)
                if full:
                    s = sp.stage(f"{tag}_div{ch}_{ti}")
                    def dv(c):
                        sc = c.e
                        sc.activation(dent[:], dent[:], AF.Ln)
                        sc.drain()
                        sc.activation(dent[:], dent[:], AF.Exp, scale=-1.0)
                    s.scalar(dv)
                    s = sp.stage(f"{tag}_wkv{ch}_{ti}")
                    s.vector(lambda c: c.e.tensor_mul(wkvt[:], numt[:], dent[:]))
                    s = sp.stage(f"{tag}_stout{ch}_{ti}")
                    s.sync(lambda c, r0=r0, off=off: c.dma(
                        wo[r0:r0 + 1024, off:off + TT]
                        .rearrange("(g p) n -> p g n", p=128),
                        wkvt[:].rearrange("p (g n) -> p g n", g=G)))
            if not full:
                s = sp.stage(f"{tag}_stx{ch}")
                def stx(c, ch=ch):
                    v = c.e
                    for g in range(G):
                        cg = ch * G + g
                        v.tensor_copy(st["st0loc"][:, cg:cg + 1],
                                      abuf[:, g * (TT + 1) + TT:
                                           g * (TT + 1) + TT + 1])
                        v.tensor_copy(st["st0loc"][:, 16 + cg:17 + cg],
                                      bbuf[:, g * (TT + 1) + TT:
                                           g * (TT + 1) + TT + 1])
                    v.tensor_copy(st["st0loc"][:, 32 + ch * G:32 + (ch + 1) * G],
                                  Mprev[:])
                s.vector(stx)


def emit_exchange(nc, sp, tag, V, pub_sb, rem_sb, sh_data, sh_flag, flag_col,
                  ctrlt, flagt, spinsem):
    nonce_col = 2 + flag_col
    s = sp.stage(tag + "_pub")
    s.gpsimd(lambda c: c.dma(sh_data[ts(V["my"], 128), :], pub_sb[:]))
    s = sp.stage(tag + "_flag")
    s.gpsimd(lambda c: c.dma(sh_flag[ts(V["my"], 1),
                             flag_col:flag_col + 1],
                             ctrlt[0:1, nonce_col:nonce_col + 1]))
    s = sp.stage(tag + "_spin")
    def spin(c):
        g = c.raw
        engs = OrderedEngineSet([g.engine])
        regs = nc.alloc_registers(tag + "_sr", engines=engs)
        nc.regs_mov(regs, 0)
        g.br(tag + "_loop")
        with nc.body(tag + "_loop", valid_engines=engs):
            g.dma_start(out=flagt[:, 0:16],
                        in_=sh_flag[ts(V["pr"], 1), :]).then_inc(spinsem, 16)
            for _ in range(64):
                g.nop()
            g.reg_load(regs[g.engine], flagt[0:1, flag_col:flag_col + 1])
            nc.br_cmp(regs, V["nonce"], tag + "_done", tag + "_loop",
                      "IS_EQ", engines=engs)
        nc.switch_bb(tag + "_done")
        c.dma(rem_sb[:], sh_data[ts(V["pr"], 128), :])
    s.gpsimd(spin)


def phase_c(nc, sp, t, io):
    xT, wkvT, srT = io["xT"], io["wkv_d"], io["sr_d"]
    woT, wgT, wuT, wdT = io["woT"], io["wgT"], io["wuT"], io["wdT"]
    x2o = io["x2_d"]
    with ExitStack() as ctx:
        sb = lambda nm, sh, dt=F32: ctx.enter_context(nc.sbuf_tensor(nm, sh, dt))
        psf = lambda nm, sh: ctx.enter_context(nc.psum_tensor(nm, sh, F32))
        xt = sb("c_xt", [128, KC * TT])
        ao = sb("c_ao", [128, KC * TT])
        zt = sb("c_zt", [128, KC * TT], BF16)
        scr = sb("c_scr", [128, 4 * TT])
        act_q = sb("c_act_q", [128, 16 * TT], BF16)
        wb = sb("c_wb", [128, KC * 512], BF16)
        wb2 = sb("c_wb2", [128, KC * 512], BF16)
        wdb = sb("c_wdb", [128, 16 * 512], BF16)
        evac = sb("c_evac", [128, 4 * TT])
        alloc_ln(nc, ctx, t, "cln", TT)
        ps_o = [psf(f"c_ps_o{i}", [128, TT]) for i in range(4)]

        for tile in range(NTILES):
            off = tile * TT
            s = sp.stage(f"c_zld{tile}")
            s.sync(lambda c, off=off: c.dma(
                xt[:].rearrange("p (kc n) -> p kc n", kc=KC),
                wkvT[:, off:off + TT].rearrange("(kc p) n -> p kc n", p=128)))
            s.sync(lambda c, off=off: c.dma(
                ao[:].rearrange("p (kc n) -> p kc n", kc=KC),
                srT[:, off:off + TT].rearrange("(kc p) n -> p kc n", p=128)))
            s = sp.stage(f"c_zmul{tile}")
            s.vector(lambda c: c.e.tensor_mul(zt[:], xt[:], ao[:]))
            for ob in range(4):
                s = sp.stage(f"c_wo{tile}_{ob}")
                s.gpsimd(lambda c, ob=ob: c.dma(
                    wb[:].rearrange("p (kc o) -> p kc o", o=512),
                    woT[:, ob * 512:(ob + 1) * 512]
                    .rearrange("(kc p) o -> p kc o", p=128)))
                s = sp.stage(f"c_womm{tile}_{ob}")
                def mms(c):
                    for oc in range(4):
                        for k in range(KC):
                            c.e.matmul(
                                ps_o[oc][:],
                                wb[:, k * 512 + oc * 128:k * 512 + (oc + 1) * 128],
                                zt[:, k * TT:(k + 1) * TT],
                                start=(k == 0), stop=(k == KC - 1))
                s.tensor(mms)
                s = sp.stage(f"c_woev{tile}_{ob}")
                for oc in range(4):
                    s.vector(lambda c, ob=ob, oc=oc: c.e.tensor_copy(
                        ao[:, (ob * 4 + oc) * TT:(ob * 4 + oc + 1) * TT],
                        ps_o[oc][:]))
            aoc = [ao[:, i * TT:(i + 1) * TT] for i in range(KC)]
            emit_ln_stats(sp, nc, t, aoc, TT, "cln")
            s = sp.stage(f"c_xld{tile}")
            s.sync(lambda c, off=off: c.dma(
                xt[:].rearrange("p (kc n) -> p kc n", kc=KC),
                xT[:, off:off + TT].rearrange("(kc p) n -> p kc n", p=128)))
            aoc_l = [ao[:, i * TT:(i + 1) * TT] for i in range(KC)]
            emit_ln_apply(sp, f"c_lna1_{tile}", aoc_l, aoc_l,
                          t["cln_mb"], t["cln_rb"], scr, TT)
            emit_res_add(sp, f"c_res1_{tile}",
                         [xt[:, i * TT:(i + 1) * TT] for i in range(KC)],
                         aoc_l)
            xc = [xt[:, i * TT:(i + 1) * TT] for i in range(KC)]
            emit_ln_stats(sp, nc, t, xc, TT, "cln")
            emit_ln_apply(sp, f"c_xln2_{tile}", xc,
                          [zt[:, i * TT:(i + 1) * TT] for i in range(KC)],
                          t["cln_mb"], t["cln_rb"], scr, TT)
            for q in range(4):
                for hb in range(4):
                    h0 = q * 2048 + hb * 512
                    s = sp.stage(f"c_fw{tile}_{q}_{hb}")
                    s.gpsimd(lambda c, h0=h0: c.dma(
                        wb[:].rearrange("p (kc o) -> p kc o", o=512),
                        wgT[:, h0:h0 + 512]
                        .rearrange("(kc p) o -> p kc o", p=128)))
                    s.gpsimd(lambda c, h0=h0: c.dma(
                        wb2[:].rearrange("p (kc o) -> p kc o", o=512),
                        wuT[:, h0:h0 + 512]
                        .rearrange("(kc p) o -> p kc o", p=128)))
                    s = sp.stage(f"c_fg{tile}_{q}_{hb}")
                    def gmm(c):
                        for oc in range(4):
                            for k in range(KC):
                                c.e.matmul(
                                    ps_o[oc][:],
                                    wb[:, k * 512 + oc * 128:
                                        k * 512 + (oc + 1) * 128],
                                    zt[:, k * TT:(k + 1) * TT],
                                    start=(k == 0), stop=(k == KC - 1))
                    s.tensor(gmm)
                    s = sp.stage(f"c_fsilu{tile}_{q}_{hb}")
                    for oc in range(4):
                        s.scalar(lambda c, oc=oc: c.e.activation(
                            evac[:, oc * TT:(oc + 1) * TT], ps_o[oc][:],
                            AF.Silu))
                    s = sp.stage(f"c_fu{tile}_{q}_{hb}")
                    def umm(c):
                        for oc in range(4):
                            for k in range(KC):
                                c.e.matmul(
                                    ps_o[oc][:],
                                    wb2[:, k * 512 + oc * 128:
                                         k * 512 + (oc + 1) * 128],
                                    zt[:, k * TT:(k + 1) * TT],
                                    start=(k == 0), stop=(k == KC - 1))
                    s.tensor(umm)
                    s = sp.stage(f"c_fact{tile}_{q}_{hb}")
                    for oc in range(4):
                        s.vector(lambda c, hb=hb, oc=oc: c.e.tensor_mul(
                            act_q[:, (hb * 4 + oc) * TT:(hb * 4 + oc + 1) * TT],
                            evac[:, oc * TT:(oc + 1) * TT], ps_o[oc][:]))
                for db in range(4):
                    s = sp.stage(f"c_fdw{tile}_{q}_{db}")
                    s.gpsimd(lambda c, q=q, db=db: c.dma(
                        wdb[:].rearrange("p (hc o) -> p hc o", o=512),
                        wdT[q * 2048:(q + 1) * 2048, db * 512:(db + 1) * 512]
                        .rearrange("(hc p) o -> p hc o", p=128)))
                    s = sp.stage(f"c_fdm{tile}_{q}_{db}")
                    def dmm(c):
                        for oc in range(4):
                            for k in range(16):
                                c.e.matmul(
                                    ps_o[oc][:],
                                    wdb[:, k * 512 + oc * 128:
                                         k * 512 + (oc + 1) * 128],
                                    act_q[:, k * TT:(k + 1) * TT],
                                    start=(k == 0), stop=(k == 15))
                    s.tensor(dmm)
                    s = sp.stage(f"c_fde{tile}_{q}_{db}")
                    for oc in range(4):
                        if q == 0:
                            s.vector(lambda c, db=db, oc=oc: c.e.tensor_copy(
                                ao[:, (db * 4 + oc) * TT:(db * 4 + oc + 1) * TT],
                                ps_o[oc][:]))
                        else:
                            s.vector(lambda c, db=db, oc=oc: c.e.tensor_add(
                                ao[:, (db * 4 + oc) * TT:(db * 4 + oc + 1) * TT],
                                ao[:, (db * 4 + oc) * TT:(db * 4 + oc + 1) * TT],
                                ps_o[oc][:]))
            fc = [ao[:, i * TT:(i + 1) * TT] for i in range(KC)]
            emit_ln_stats(sp, nc, t, fc, TT, "cln")
            emit_ln_apply(sp, f"c_lna2_{tile}", fc, fc,
                          t["cln_mb"], t["cln_rb"], scr, TT)
            emit_res_add(sp, f"c_res2_{tile}",
                         [xt[:, i * TT:(i + 1) * TT] for i in range(KC)], fc)
            s = sp.stage(f"c_x2st{tile}")
            s.sync(lambda c, off=off: c.dma(
                x2o[:, off:off + TT].rearrange("(kc p) n -> p kc n", p=128),
                xt[:].rearrange("p (kc n) -> p kc n", kc=KC)))


def phase_d(nc, sp, t, io):
    xT = io["x2_d"]
    wq, cosd, sind, rotd = io["wqkvT"], io["cosd"], io["sind"], io["rotP"]
    qo, ko, vo = io["q_d"], io["k1_d"], io["v1_d"]
    with ExitStack() as ctx:
        sb = lambda nm, sh, dt=F32: ctx.enter_context(nc.sbuf_tensor(nm, sh, dt))
        psf = lambda nm, sh, dt=F32: ctx.enter_context(
            nc.psum_tensor(nm, sh, dt))
        xt = sb("d_xt", [128, KC * TT])
        xln = sb("d_xln", [128, KC * TT], BF16)
        scr = sb("d_scr", [128, 4 * TT])
        emin = sb("d_emin", [128, 4 * TT])
        eexp = sb("d_eexp", [128, 4 * TT])
        cost = sb("d_cost", [128, TT])
        sint = sb("d_sint", [128, TT])
        rott = sb("d_rott", [128, 128], BF16)
        wb = sb("d_wb", [128, KC * 512], BF16)
        qkb = sb("d_qkb", [128, 4 * TT], BF16)
        ebuf = sb("d_ebuf", [128, 4 * TT])
        outb = sb("d_outb", [128, 4 * TT], BF16)
        alloc_ln(nc, ctx, t, "dln", TT)
        ps_o = [psf(f"d_ps_o{i}", [128, TT]) for i in range(4)]
        s = sp.stage("d_rot_load")
        s.gpsimd(lambda c: c.dma(rott[:], rotd))

        for tile in range(NTILES):
            off = tile * TT
            s = sp.stage(f"d_xld{tile}")
            s.sync(lambda c, off=off: c.dma(
                xt[:].rearrange("p (kc n) -> p kc n", kc=KC),
                xT[:, off:off + TT].rearrange("(kc p) n -> p kc n", p=128)))
            s.sync(lambda c, off=off: c.dma(cost[:], cosd[:, off:off + TT]))
            s.sync(lambda c, off=off: c.dma(sint[:], sind[:, off:off + TT]))
            xc = [xt[:, i * TT:(i + 1) * TT] for i in range(KC)]
            emit_ln_stats(sp, nc, t, xc, TT, "dln")
            emit_ln_apply(sp, f"d_xln{tile}", xc,
                          [xln[:, i * TT:(i + 1) * TT] for i in range(KC)],
                          t["dln_mb"], t["dln_rb"], scr, TT)
            for ob in range(12):
                kind = ob // 4
                s = sp.stage(f"d_w{tile}_{ob}")
                s.gpsimd(lambda c, ob=ob: c.dma(
                    wb[:].rearrange("p (kc o) -> p kc o", o=512),
                    wq[:, ob * 512:(ob + 1) * 512]
                    .rearrange("(kc p) o -> p kc o", p=128)))
                s = sp.stage(f"d_mm{tile}_{ob}")
                def mms(c):
                    for oc in range(4):
                        for k in range(KC):
                            c.e.matmul(
                                ps_o[oc][:],
                                wb[:, k * 512 + oc * 128:k * 512 + (oc + 1) * 128],
                                xln[:, k * TT:(k + 1) * TT],
                                start=(k == 0), stop=(k == KC - 1))
                s.tensor(mms)
                if kind == 2:
                    s = sp.stage(f"d_vev{tile}_{ob}")
                    for oc in range(4):
                        s.vector(lambda c, oc=oc: c.e.tensor_copy(
                            outb[:, oc * TT:(oc + 1) * TT], ps_o[oc][:]))
                    s = sp.stage(f"d_vst{tile}_{ob}")
                    s.sync(lambda c, ob=ob, off=off: c.dma(
                        vo[(ob - 8) * 512:(ob - 7) * 512, off:off + TT]
                        .rearrange("(oc p) n -> p oc n", p=128),
                        outb[:].rearrange("p (oc n) -> p oc n", oc=4)))
                else:
                    s = sp.stage(f"d_qe{tile}_{ob}")
                    for oc in range(4):
                        s.vector(lambda c, oc=oc: c.e.tensor_copy(
                            ebuf[:, oc * TT:(oc + 1) * TT], ps_o[oc][:]))
                    s = sp.stage(f"d_qb{tile}_{ob}")
                    s.vector(lambda c: c.e.tensor_copy(qkb[:], ebuf[:]))
                    s = sp.stage(f"d_rot{tile}_{ob}")
                    def rmm(c):
                        for oc in range(4):
                            c.e.matmul(ps_o[oc][:], rott[:],
                                       qkb[:, oc * TT:(oc + 1) * TT],
                                       start=True, stop=True)
                    s.tensor(rmm)
                    s = sp.stage(f"d_rope{tile}_{ob}")
                    def rope(c):
                        v = c.e
                        for pair in range(2):
                            o0, o1 = 2 * pair, 2 * pair + 1
                            s0 = slice(o0 * TT, (o0 + 1) * TT)
                            s1 = slice(o1 * TT, (o1 + 1) * TT)
                            v.tensor_mul(scr[:, 0:TT], ebuf[:, s0], cost[:])
                            v.tensor_mul(scr[:, TT:2 * TT], ps_o[o0][:],
                                         sint[:])
                            v.tensor_mul(scr[:, 2 * TT:3 * TT], ebuf[:, s1],
                                         cost[:])
                            v.tensor_mul(scr[:, 3 * TT:4 * TT], ps_o[o1][:],
                                         sint[:])
                            v.drain()
                            v.tensor_add(ebuf[:, s0], scr[:, 0:TT],
                                         scr[:, TT:2 * TT])
                            v.tensor_add(ebuf[:, s1], scr[:, 2 * TT:3 * TT],
                                         scr[:, 3 * TT:4 * TT])
                            v.drain()
                            v.tensor_scalar_min(emin[:, s0], ebuf[:, s0], 0.0)
                            v.tensor_scalar_max(ebuf[:, s0], ebuf[:, s0], 0.0)
                            v.tensor_scalar_min(emin[:, s1], ebuf[:, s1], 0.0)
                            v.tensor_scalar_max(ebuf[:, s1], ebuf[:, s1], 0.0)
                    s.vector(rope)
                    s = sp.stage(f"d_elu{tile}_{ob}")
                    for oc in range(4):
                        s.scalar(lambda c, oc=oc: c.e.activation(
                            eexp[:, oc * TT:(oc + 1) * TT],
                            emin[:, oc * TT:(oc + 1) * TT], AF.Exp))
                    s = sp.stage(f"d_elu2{tile}_{ob}")
                    for oc in range(4):
                        s.vector(lambda c, oc=oc: c.e.tensor_add(
                            outb[:, oc * TT:(oc + 1) * TT],
                            ebuf[:, oc * TT:(oc + 1) * TT],
                            eexp[:, oc * TT:(oc + 1) * TT]))
                    s = sp.stage(f"d_qst{tile}_{ob}")
                    outd = qo if kind == 0 else ko
                    bb = ob - 4 * kind
                    s.sync(lambda c, outd=outd, bb=bb, off=off: c.dma(
                        outd[bb * 512:(bb + 1) * 512, off:off + TT]
                        .rearrange("(oc p) n -> p oc n", p=128),
                        outb[:].rearrange("p (oc n) -> p oc n", oc=4)))


def phase_e1(nc, sp, t, io, st1loc):
    T = TPC
    kd, vd = io["k1_d"], io["v1_d"]
    identd = io["ident"]
    with ExitStack() as ctx:
        sb = lambda nm, sh, dt=F32: ctx.enter_context(nc.sbuf_tensor(nm, sh, dt))
        psf = lambda nm, sh, dt=F32: ctx.enter_context(
            nc.psum_tensor(nm, sh, dt))
        kt = sb("e1_kt", [128, T], BF16)
        vt = sb("e1_vt", [128, T], BF16)
        identt = sb("e1_identt", [128, 128], BF16)
        tpb = sb("e1_tpb", [128, 8 * 128], BF16)
        ps_tp = psf("e1_ps_tp", [128, 8 * 128], BF16)
        ps_S = psf("e1_ps_S", [128, HDIM])
        s = sp.stage("e1_const")
        s.gpsimd(lambda c: c.dma(identt[:], identd))
        for h in range(NHEAD):
            s = sp.stage(f"e1_ld{h}")
            s.sync(lambda c, h=h: c.dma(kt[:], kd[h * 128:(h + 1) * 128, :]))
            s.sync(lambda c, h=h: c.dma(vt[:], vd[h * 128:(h + 1) * 128, :]))
            for grp in range(4):
                s = sp.stage(f"e1_tr{h}_{grp}")
                def tr(c, grp=grp):
                    e = c.e
                    for j in range(4):
                        c0 = (grp * 4 + j) * 128
                        e.transpose(ps_tp[:, j * 128:(j + 1) * 128],
                                    kt[:, c0:c0 + 128], identt[:])
                        e.transpose(ps_tp[:, (4 + j) * 128:(5 + j) * 128],
                                    vt[:, c0:c0 + 128], identt[:])
                s.tensor(tr)
                s = sp.stage(f"e1_ev{h}_{grp}")
                s.vector(lambda c: c.e.tensor_copy(tpb[:], ps_tp[:]))
                s = sp.stage(f"e1_mm{h}_{grp}")
                def mm(c, grp=grp):
                    for j in range(4):
                        c.e.matmul(ps_S[:], tpb[:, j * 128:(j + 1) * 128],
                                   tpb[:, (4 + j) * 128:(5 + j) * 128],
                                   start=(grp == 0 and j == 0),
                                   stop=(grp == 3 and j == 3))
                s.tensor(mm)
            s = sp.stage(f"e1_fin{h}")
            def fin(c, h=h):
                v = c.e
                v.tensor_copy(st1loc[:, h * 129:h * 129 + 128], ps_S[:])
                v.reduce_sum(st1loc[:, h * 129 + 128:h * 129 + 129], kt[:],
                             axis=AX)
            s.vector(fin)


def phase_e2(nc, sp, t, io, st1rem, selt):
    T = TPC
    NCH = T // CHUNK
    qd, kd, vd = io["q_d"], io["k1_d"], io["v1_d"]
    trild = io["trilT"]
    identd = io["ident"]
    oo = io["ho_d"]
    with ExitStack() as ctx:
        sb = lambda nm, sh, dt=F32: ctx.enter_context(nc.sbuf_tensor(nm, sh, dt))
        psf = lambda nm, sh, dt=F32: ctx.enter_context(
            nc.psum_tensor(nm, sh, dt))
        qt = sb("e2_qt", [128, T], BF16)
        kt = sb("e2_kt", [128, T], BF16)
        vt = sb("e2_vt", [128, T], BF16)
        trilt = sb("e2_trilt", [128, 2 * CHUNK])
        identt = sb("e2_identt", [128, 128], BF16)
        onesr = sb("e2_onesr", [1, 128], F32)
        onesb = sb("e2_onesb", [128, 1], BF16)
        attb = sb("e2_attb", [128, 2 * CHUNK], BF16)
        tpb = sb("e2_tpb", [128, 4 * 128], BF16)
        Ssb = sb("e2_Ssb", [128, HDIM])
        Sb16 = sb("e2_Sb16", [128, HDIM], BF16)
        Zsb = sb("e2_Zsb", [128, 1])
        Zb16 = sb("e2_Zb16", [128, 1], BF16)
        zscr = sb("e2_zscr", [128, 1])
        numsb = sb("e2_numsb", [128, T])
        densb = sb("e2_densb", [1, T])
        outhb = sb("e2_outhb", [128, T], BF16)
        ps_att = psf("e2_ps_att", [128, 2 * CHUNK])
        ps_tp = psf("e2_ps_tp", [128, 4 * 128], BF16)
        ps_num = psf("e2_ps_num", [128, CHUNK])
        ps_den = psf("e2_ps_den", [1, CHUNK])
        ps_S = psf("e2_ps_S", [128, HDIM])
        ps_rec = psf("e2_ps_rec", [128, TT])
        s = sp.stage("e2_const")
        s.sync(lambda c: c.dma(trilt[:].rearrange("p (mc l) -> p mc l", mc=2),
                               trild.rearrange("(mc p) l -> p mc l", p=128)))
        s.sync(lambda c: c.dma(onesr[:], io["ones_row"]))
        s.gpsimd(lambda c: c.dma(identt[:], identd))
        s.gpsimd(lambda c: c.dma(onesb[:],
                                 io["ones_row"].rearrange("one p -> p one"),
                                 slow=True))

        for h in range(NHEAD):
            s = sp.stage(f"e2_hld{h}")
            s.sync(lambda c, h=h: c.dma(qt[:], qd[h * 128:(h + 1) * 128, :]))
            s.sync(lambda c, h=h: c.dma(kt[:], kd[h * 128:(h + 1) * 128, :]))
            s.sync(lambda c, h=h: c.dma(vt[:], vd[h * 128:(h + 1) * 128, :]))
            def init(c, h=h):
                v = c.e
                v.tensor_scalar_mul(Ssb[:], st1rem[:, h * 129:h * 129 + 128],
                                    selt[:])
                v.tensor_scalar_mul(Zsb[:],
                                    st1rem[:, h * 129 + 128:h * 129 + 129],
                                    selt[:])
                v.drain()
                v.tensor_copy(Sb16[:], Ssb[:])
                v.tensor_copy(Zb16[:], Zsb[:])
            s.vector(init)
            for ch in range(NCH):
                c0 = ch * CHUNK
                qc = qt[:, c0:c0 + CHUNK]
                kc = kt[:, c0:c0 + CHUNK]
                vc = vt[:, c0:c0 + CHUNK]
                s = sp.stage(f"e2_att{h}_{ch}")
                def amm(c, qc=qc, kc=kc, vc=vc):
                    e = c.e
                    for mc in range(2):
                        e.matmul(ps_att[:, mc * CHUNK:(mc + 1) * CHUNK],
                                 kc[:, mc * 128:(mc + 1) * 128], qc,
                                 start=True, stop=True)
                    for mc in range(2):
                        e.transpose(ps_tp[:, mc * 128:(mc + 1) * 128],
                                    kc[:, mc * 128:(mc + 1) * 128], identt[:])
                        e.transpose(ps_tp[:, (2 + mc) * 128:(3 + mc) * 128],
                                    vc[:, mc * 128:(mc + 1) * 128], identt[:])
                s.tensor(amm)
                s = sp.stage(f"e2_mask{h}_{ch}")
                def mask(c):
                    v = c.e
                    for mc in range(2):
                        v.tensor_mul(attb[:, mc * CHUNK:(mc + 1) * CHUNK],
                                     ps_att[:, mc * CHUNK:(mc + 1) * CHUNK],
                                     trilt[:, mc * CHUNK:(mc + 1) * CHUNK])
                    v.tensor_copy(tpb[:], ps_tp[:])
                s.vector(mask)
                s = sp.stage(f"e2_num{h}_{ch}")
                def nmm(c, qc=qc):
                    e = c.e
                    for mc in range(2):
                        e.matmul(ps_num[:], tpb[:, (2 + mc) * 128:(3 + mc) * 128],
                                 attb[:, mc * CHUNK:(mc + 1) * CHUNK],
                                 start=(mc == 0), stop=False)
                    e.matmul(ps_num[:], Sb16[:], qc, start=False, stop=True)
                    for mc in range(2):
                        e.matmul(ps_den[:], onesb[:],
                                 attb[:, mc * CHUNK:(mc + 1) * CHUNK],
                                 start=(mc == 0), stop=False)
                    e.matmul(ps_den[:], Zb16[:], qc, start=False, stop=True)
                    for mc in range(2):
                        e.matmul(ps_S[:], tpb[:, mc * 128:(mc + 1) * 128],
                                 tpb[:, (2 + mc) * 128:(3 + mc) * 128],
                                 start=(mc == 0), stop=(mc == 1))
                s.tensor(nmm)
                s = sp.stage(f"e2_upd{h}_{ch}")
                def upd(c, ch=ch, kc=kc, c0=c0):
                    v = c.e
                    v.tensor_copy(numsb[:, c0:c0 + CHUNK], ps_num[:])
                    v.tensor_copy(densb[0:1, c0:c0 + CHUNK], ps_den[:])
                    v.reduce_sum(zscr[:], kc, axis=AX)
                    v.tensor_add(Ssb[:], Ssb[:], ps_S[:])
                    v.drain()
                    v.tensor_add(Zsb[:], Zsb[:], zscr[:])
                    v.drain()
                    v.tensor_copy(Sb16[:], Ssb[:])
                    v.tensor_copy(Zb16[:], Zsb[:])
                s.vector(upd)
            s = sp.stage(f"e2_dmax{h}")
            s.vector(lambda c: c.e.tensor_scalar_max(densb[:], densb[:], 1e-6))
            s = sp.stage(f"e2_dln{h}")
            def dln(c):
                c.e.activation(densb[:], densb[:], AF.Ln)
                c.e.drain()
                c.e.activation(densb[:], densb[:], AF.Exp, scale=-1.0)
            s.scalar(dln)
            for tb in range(T // TT):
                s = sp.stage(f"e2_rb{h}_{tb}")
                s.tensor(lambda c, tb=tb: c.e.matmul(
                    ps_rec[:], onesr[:], densb[0:1, tb * TT:(tb + 1) * TT],
                    start=True, stop=True))
                s = sp.stage(f"e2_ro{h}_{tb}")
                s.vector(lambda c, tb=tb: c.e.tensor_mul(
                    outhb[:, tb * TT:(tb + 1) * TT],
                    numsb[:, tb * TT:(tb + 1) * TT], ps_rec[:]))
            s = sp.stage(f"e2_hst{h}")
            s.sync(lambda c, h=h: c.dma(oo[h * 128:(h + 1) * 128, :],
                                        outhb[:]))


def phase_f(nc, sp, t, io):
    hoT, xT = io["ho_d"], io["x2_d"]
    woT, w0T, b0d, w1T, b1d = (io["woutT"], io["w0T"], io["b0"],
                               io["w1T"], io["b1"])
    xo = io["x_out"]
    QH = 1024
    NQ = HID // QH
    with ExitStack() as ctx:
        sb = lambda nm, sh, dt=F32: ctx.enter_context(nc.sbuf_tensor(nm, sh, dt))
        psf = lambda nm, sh, dt=F32: ctx.enter_context(
            nc.psum_tensor(nm, sh, dt))
        xt = sb("f_xt", [128, KC * TT])
        hob = sb("f_hob", [128, KC * TT], BF16)
        ao = sb("f_ao", [128, KC * TT])
        xln = sb("f_xln", [128, KC * TT], BF16)
        scr = sb("f_scr", [128, 4 * TT])
        act_q = sb("f_act_q", [128, 8 * TT], BF16)
        actf4 = sb("f_actf4", [128, 4 * TT], F16)
        facf = sb("f_facf", [128, KC * TT], F16)
        b0t = sb("f_b0t", [128, 64])
        b1t = sb("f_b1t", [128, 16])
        wb = sb("f_wb", [128, KC * 512], BF16)
        wdb = sb("f_wdb", [128, 8 * 512], BF16)
        alloc_ln(nc, ctx, t, "fln", TT)
        ps_o = [psf(f"f_ps_o{i}", [128, TT]) for i in range(4)]
        s = sp.stage("f_bias_load")
        s.sync(lambda c: c.dma(b0t[:].rearrange("p (hc one) -> p hc one", one=1),
                               b0d.rearrange("(hc p) one -> p hc one", p=128),
                               slow=True))
        s.sync(lambda c: c.dma(b1t[:].rearrange("p (kc one) -> p kc one", one=1),
                               b1d.rearrange("(kc p) one -> p kc one", p=128),
                               slow=True))

        for tile in range(NTILES):
            off = tile * TT
            s = sp.stage(f"f_ld{tile}")
            s.sync(lambda c, off=off: c.dma(
                hob[:].rearrange("p (kc n) -> p kc n", kc=KC),
                hoT[:, off:off + TT].rearrange("(kc p) n -> p kc n", p=128)))
            s.sync(lambda c, off=off: c.dma(
                xt[:].rearrange("p (kc n) -> p kc n", kc=KC),
                xT[:, off:off + TT].rearrange("(kc p) n -> p kc n", p=128)))
            for ob in range(4):
                s = sp.stage(f"f_wo{tile}_{ob}")
                s.gpsimd(lambda c, ob=ob: c.dma(
                    wb[:].rearrange("p (kc o) -> p kc o", o=512),
                    woT[:, ob * 512:(ob + 1) * 512]
                    .rearrange("(kc p) o -> p kc o", p=128)))
                s = sp.stage(f"f_womm{tile}_{ob}")
                def mms(c):
                    for oc in range(4):
                        for k in range(KC):
                            c.e.matmul(
                                ps_o[oc][:],
                                wb[:, k * 512 + oc * 128:k * 512 + (oc + 1) * 128],
                                hob[:, k * TT:(k + 1) * TT],
                                start=(k == 0), stop=(k == KC - 1))
                s.tensor(mms)
                s = sp.stage(f"f_woev{tile}_{ob}")
                for oc in range(4):
                    s.vector(lambda c, ob=ob, oc=oc: c.e.tensor_copy(
                        ao[:, (ob * 4 + oc) * TT:(ob * 4 + oc + 1) * TT],
                        ps_o[oc][:]))
            aoc = [ao[:, i * TT:(i + 1) * TT] for i in range(KC)]
            emit_ln_stats(sp, nc, t, aoc, TT, "fln")
            emit_ln_apply(sp, f"f_lna1_{tile}", aoc, aoc,
                          t["fln_mb"], t["fln_rb"], scr, TT)
            emit_res_add(sp, f"f_res1_{tile}",
                         [xt[:, i * TT:(i + 1) * TT] for i in range(KC)], aoc)
            xc = [xt[:, i * TT:(i + 1) * TT] for i in range(KC)]
            emit_ln_stats(sp, nc, t, xc, TT, "fln")
            emit_ln_apply(sp, f"f_xln{tile}", xc,
                          [xln[:, i * TT:(i + 1) * TT] for i in range(KC)],
                          t["fln_mb"], t["fln_rb"], scr, TT)
            for q in range(NQ):
                for hb in range(2):
                    h0 = q * QH + hb * 512
                    s = sp.stage(f"f_fw{tile}_{q}_{hb}")
                    s.gpsimd(lambda c, h0=h0: c.dma(
                        wb[:].rearrange("p (kc o) -> p kc o", o=512),
                        w0T[:, h0:h0 + 512]
                        .rearrange("(kc p) o -> p kc o", p=128)))
                    s = sp.stage(f"f_f0{tile}_{q}_{hb}")
                    def gmm(c):
                        for oc in range(4):
                            for k in range(KC):
                                c.e.matmul(
                                    ps_o[oc][:],
                                    wb[:, k * 512 + oc * 128:
                                        k * 512 + (oc + 1) * 128],
                                    xln[:, k * TT:(k + 1) * TT],
                                    start=(k == 0), stop=(k == KC - 1))
                    s.tensor(gmm)
                    s = sp.stage(f"f_fgel{tile}_{q}_{hb}")
                    for oc in range(4):
                        def gel(c, q=q, hb=hb, oc=oc):
                            hc = q * 8 + hb * 4 + oc
                            c.e.activation(actf4[:, oc * TT:(oc + 1) * TT],
                                           ps_o[oc][:], AF.Gelu,
                                           bias=b0t[:, hc:hc + 1])
                        s.scalar(gel)
                    s = sp.stage(f"f_fcb{tile}_{q}_{hb}")
                    for oc in range(4):
                        s.vector(lambda c, hb=hb, oc=oc: c.e.tensor_copy(
                            act_q[:, (hb * 4 + oc) * TT:(hb * 4 + oc + 1) * TT],
                            actf4[:, oc * TT:(oc + 1) * TT]))
                for db in range(4):
                    s = sp.stage(f"f_fdw{tile}_{q}_{db}")
                    s.gpsimd(lambda c, q=q, db=db: c.dma(
                        wdb[:].rearrange("p (hc o) -> p hc o", o=512),
                        w1T[q * QH:(q + 1) * QH, db * 512:(db + 1) * 512]
                        .rearrange("(hc p) o -> p hc o", p=128)))
                    s = sp.stage(f"f_fdm{tile}_{q}_{db}")
                    def dmm(c):
                        for oc in range(4):
                            for k in range(8):
                                c.e.matmul(
                                    ps_o[oc][:],
                                    wdb[:, k * 512 + oc * 128:
                                         k * 512 + (oc + 1) * 128],
                                    act_q[:, k * TT:(k + 1) * TT],
                                    start=(k == 0), stop=(k == 7))
                    s.tensor(dmm)
                    s = sp.stage(f"f_fde{tile}_{q}_{db}")
                    for oc in range(4):
                        if q == 0:
                            s.vector(lambda c, db=db, oc=oc: c.e.tensor_copy(
                                ao[:, (db * 4 + oc) * TT:(db * 4 + oc + 1) * TT],
                                ps_o[oc][:]))
                        else:
                            s.vector(lambda c, db=db, oc=oc: c.e.tensor_add(
                                ao[:, (db * 4 + oc) * TT:(db * 4 + oc + 1) * TT],
                                ao[:, (db * 4 + oc) * TT:(db * 4 + oc + 1) * TT],
                                ps_o[oc][:]))
            s = sp.stage(f"f_fb1_{tile}")
            def b1a(c):
                v = c.e
                for r in range(4):
                    for j in range(4):
                        i = r * 4 + j
                        v.tensor_scalar_add(ao[:, i * TT:(i + 1) * TT],
                                            ao[:, i * TT:(i + 1) * TT],
                                            b1t[:, i:i + 1])
                    v.drain()
                    for j in range(4):
                        i = r * 4 + j
                        v.tensor_copy(facf[:, i * TT:(i + 1) * TT],
                                      ao[:, i * TT:(i + 1) * TT])
            s.vector(b1a)
            fc = [facf[:, i * TT:(i + 1) * TT] for i in range(KC)]
            emit_ln_stats(sp, nc, t, fc, TT, "fln")
            emit_ln_apply(sp, f"f_lna2_{tile}", fc, fc,
                          t["fln_mb"], t["fln_rb"], scr, TT)
            emit_res_add(sp, f"f_res2_{tile}",
                         [xt[:, i * TT:(i + 1) * TT] for i in range(KC)], fc)
            s = sp.stage(f"f_xst{tile}")
            s.sync(lambda c, off=off: c.dma(
                xo[:, off:off + TT].rearrange("(kc p) n -> p kc n", p=128),
                xt[:].rearrange("p (kc n) -> p kc n", kc=KC)))


def build_fused():
    nc = bass.Bass("TRN2", target_bir_lowering=False,
                   detect_race_conditions=DETECT_RACE)
    io = {}
    def ein(nm, sh, dt=F32):
        io[nm] = nc.dram_tensor(nm, sh, dt, kind="ExternalInput").ap()
    ein("xT", [D, TPC]); ein("prev", [D, 1])
    ein("wkT", [D, D]); ein("wvT", [D, D]); ein("wrT", [D, D])
    ein("tm", [D, 3]); ein("td", [D, 1]); ein("tf", [D, 1])
    ein("woT", [D, D]); ein("wgT", [D, HID]); ein("wuT", [D, HID])
    ein("wdT", [HID, D])
    ein("wqkvT", [D, 3 * D]); ein("cosd", [128, TPC]); ein("sind", [128, TPC])
    ein("rotP", [128, 128]); ein("trilT", [CHUNK, CHUNK])
    ein("ident", [128, 128])
    ein("woutT", [D, D]); ein("w0T", [D, HID]); ein("b0", [HID, 1])
    ein("w1T", [HID, D]); ein("b1", [D, 1])
    ein("ones_col", [128, 1]); ein("ones_row", [1, 128])
    ein("ctrl", [1, 8], I32); ein("sel", [128, 1])
    io["x_out"] = nc.dram_tensor("x_out", [D, TPC], F32,
                                 kind="ExternalOutput").ap()
    def internal(nm, sh, dt=F32):
        io[nm] = nc.dram_tensor(nm, sh, dt).ap()
    internal("k_d", [D, TPC]); internal("v_d", [D, TPC])
    internal("sr_d", [D, TPC]); internal("wkv_d", [D, TPC])
    internal("x2_d", [D, TPC])
    internal("q_d", [D, TPC], BF16); internal("k1_d", [D, TPC], BF16)
    internal("v1_d", [D, TPC], BF16); internal("ho_d", [D, TPC], BF16)
    sh_st0 = nc.dram_tensor("sh_st0", [256, 48], F32,
                            addr_space="Shared").ap()
    sh_st1 = nc.dram_tensor("sh_st1", [256, NHEAD * 129], F32,
                            addr_space="Shared").ap()
    sh_flag = nc.dram_tensor("sh_flag", [2, 16], I32,
                             addr_space="Shared").ap()

    with ExitStack() as octx:
        t = alloc_common(nc, octx)
        sb = lambda nm, sh, dt=F32: octx.enter_context(
            nc.sbuf_tensor(nm, sh, dt))
        ctrlt = sb("ctrlt", [1, 8], I32)
        selt = sb("selt", [128, 1])
        flagt = sb("flagt", [1, 16], I32)
        spinsem = octx.enter_context(nc.semaphore("spinsem"))
        V = {}

        sp = SP(nc)
        s = sp.stage("load_common")
        s.gpsimd(lambda c: c.dma(t["ones_col_b"][:], io["ones_col"]))
        s.sync(lambda c: c.dma(t["ones_row_f"][:], io["ones_row"]))
        s.sync(lambda c: c.dma(ctrlt[:], io["ctrl"]))
        s.sync(lambda c: c.dma(selt[:], io["sel"]))
        s = sp.stage("ctrl_vals")
        def cv(c):
            g = c.raw
            sk = dict(engines=[g.engine], skip_runtime_bounds_check=True)
            V["my"] = nc.values_load(ctrlt[0:1, 0:1], **sk)
            V["pr"] = nc.values_load(ctrlt[0:1, 1:2], **sk)
        s.gpsimd(cv)

        phase_a(nc, sp, t, io)
        with ExitStack() as bctx:
            bsb = lambda nm, sh: bctx.enter_context(
                nc.sbuf_tensor(nm, sh, F32))
            st0loc = bsb("st0loc", [128, 48])
            st0rem = bsb("st0rem", [128, 48])
            stAm = bsb("stAm", [128, 16])
            stBm = bsb("stBm", [128, 16])
            st = {"st0loc": st0loc, "stAm": stAm, "stBm": stBm,
                  "stE": st0rem[:, 32:48]}
            phase_scan(nc, sp, t, io, "b1", False, st)

            def set_nonce0(c):
                g = c.raw
                V["nonce"] = nc.values_load(
                    ctrlt[0:1, 2:3], engines=[g.engine],
                    skip_runtime_bounds_check=True)
            s = sp.stage("nonce0")
            s.gpsimd(set_nonce0)
            emit_exchange(nc, sp, "x0", V, st0loc, st0rem, sh_st0, sh_flag, 0,
                          ctrlt, flagt, spinsem)
            s = sp.stage("b2_mask")
            def msk(c):
                v = c.e
                v.tensor_scalar_mul(stAm[:], st0rem[:, 0:16], selt[:])
                v.tensor_scalar_mul(stBm[:], st0rem[:, 16:32], selt[:])
            s.vector(msk)
            phase_scan(nc, sp, t, io, "b2", True, st)
        phase_c(nc, sp, t, io)
        phase_d(nc, sp, t, io)
        with ExitStack() as ectx:
            st1loc = ectx.enter_context(
                nc.sbuf_tensor("st1loc", [128, NHEAD * 129], F32))
            st1rem = ectx.enter_context(
                nc.sbuf_tensor("st1rem", [128, NHEAD * 129], F32))
            phase_e1(nc, sp, t, io, st1loc)

            def set_nonce1(c):
                g = c.raw
                V["nonce"] = nc.values_load(
                    ctrlt[0:1, 3:4], engines=[g.engine],
                    skip_runtime_bounds_check=True)
            s = sp.stage("nonce1")
            s.gpsimd(set_nonce1)
            emit_exchange(nc, sp, "x1", V, st1loc, st1rem, sh_st1, sh_flag, 1,
                          ctrlt, flagt, spinsem)
            phase_e2(nc, sp, t, io, st1rem, selt)
        phase_f(nc, sp, t, io)
        sp.emit(octx)
    return nc


def _f32c(a):
    return np.ascontiguousarray(np.asarray(a, np.float32))


def kernel(L0_attn_x, L0_attn_time_decay, L0_attn_time_first,
           L0_attn_time_mix_k, L0_attn_time_mix_v, L0_attn_time_mix_r,
           L0_attn_Wk, L0_attn_Wv, L0_attn_Wr, L0_attn_Wo,
           L0_ffn_w_gate, L0_ffn_w_up, L0_ffn_w_down,
           L1_attn_W_qkv, L1_attn_W_out,
           L1_ffn_W0, L1_ffn_bias0, L1_ffn_W1, L1_ffn_bias1):
    B, T = 4, 4096
    x = _f32c(L0_attn_x)
    xTc = [np.ascontiguousarray(x[c // 2, (c % 2) * TPC:(c % 2 + 1) * TPC].T)
           for c in range(NCORES)]
    prevs = [np.zeros((D, 1), np.float32) if c % 2 == 0 else
             np.ascontiguousarray(x[c // 2, TPC - 1:TPC].T)
             for c in range(NCORES)]
    ones_col = np.ones((128, 1), np.float32)
    ones_row = np.ones((1, 128), np.float32)
    wkT = np.ascontiguousarray(_f32c(L0_attn_Wk).T)
    wvT = np.ascontiguousarray(_f32c(L0_attn_Wv).T)
    wrT = np.ascontiguousarray(_f32c(L0_attn_Wr).T)
    tm = np.ascontiguousarray(np.stack(
        [_f32c(L0_attn_time_mix_k).reshape(D),
         _f32c(L0_attn_time_mix_v).reshape(D),
         _f32c(L0_attn_time_mix_r).reshape(D)], axis=1))
    td = _f32c(L0_attn_time_decay).reshape(D, 1)
    tf = _f32c(L0_attn_time_first).reshape(D, 1)
    woT = np.ascontiguousarray(_f32c(L0_attn_Wo).T)
    wgT = np.ascontiguousarray(_f32c(L0_ffn_w_gate).T)
    wuT = np.ascontiguousarray(_f32c(L0_ffn_w_up).T)
    wdT = np.ascontiguousarray(_f32c(L0_ffn_w_down).T)
    wqkvT = np.ascontiguousarray(_f32c(L1_attn_W_qkv).T)
    inv = 1.0 / (10000.0 ** (np.arange(0, HDIM, 2, dtype=np.float32) / HDIM))
    pos = np.arange(T, dtype=np.float32)
    fr = np.outer(pos, inv)
    emb = np.concatenate([fr, fr], axis=-1)
    cosT = np.ascontiguousarray(np.cos(emb).T.astype(np.float32))
    sinT = np.ascontiguousarray(np.sin(emb).T.astype(np.float32))
    P = np.zeros((128, 128), np.float32)
    for i in range(64):
        P[2 * i, 2 * i + 1] = -1.0
        P[2 * i + 1, 2 * i] = 1.0
    rotP = np.ascontiguousarray(P.T)
    trilT = np.ascontiguousarray(np.triu(np.ones((CHUNK, CHUNK), np.float32)))
    ident = np.eye(128, dtype=np.float32)
    woutT = np.ascontiguousarray(_f32c(L1_attn_W_out).T)
    w0T = np.ascontiguousarray(_f32c(L1_ffn_W0).T)
    w1T = np.ascontiguousarray(_f32c(L1_ffn_W1).T)
    b0 = _f32c(L1_ffn_bias0).reshape(HID, 1)
    b1 = _f32c(L1_ffn_bias1).reshape(D, 1)
    rng = np.random.default_rng()
    nonce0 = int(rng.integers(1, 2**30))
    nonce1 = nonce0 + 1

    nc = build_fused()
    maps = []
    for c in range(NCORES):
        hf = c % 2
        sel = np.full((128, 1), float(hf), np.float32)
        ctrl = np.array([[hf, 1 - hf, nonce0, nonce1, 0, 0, 0, 0]], np.int32)
        maps.append({
            "xT": xTc[c], "prev": prevs[c],
            "wkT": wkT, "wvT": wvT, "wrT": wrT, "tm": tm,
            "td": td, "tf": tf,
            "woT": woT, "wgT": wgT, "wuT": wuT, "wdT": wdT,
            "wqkvT": wqkvT,
            "cosd": np.ascontiguousarray(cosT[:, hf * TPC:(hf + 1) * TPC]),
            "sind": np.ascontiguousarray(sinT[:, hf * TPC:(hf + 1) * TPC]),
            "rotP": rotP, "trilT": trilT, "ident": ident,
            "woutT": woutT, "w0T": w0T, "b0": b0, "w1T": w1T, "b1": b1,
            "ones_col": ones_col, "ones_row": ones_row,
            "ctrl": ctrl, "sel": sel,
        })
    res = run_bass_kernel_spmd(nc, maps, core_ids=list(range(NCORES)))
    out = np.zeros((B, T, D), np.float32)
    for c in range(NCORES):
        b, hf = c // 2, c % 2
        out[b, hf * TPC:(hf + 1) * TPC] = np.asarray(
            res.results[c]["x_out"]).T
    return out


# version 06 (final)
# speedup vs baseline: 1.7045x; 1.7045x over previous
import numpy as np
import ml_dtypes
from contextlib import ExitStack

import concourse.bass as bass
import concourse.mybir as mybir
from concourse.bass import ts, OrderedEngineSet
from concourse.bass_utils import run_bass_kernel_spmd

F32 = mybir.dt.float32
BF16 = mybir.dt.bfloat16
F16 = mybir.dt.float16
I32 = mybir.dt.int32
AF = mybir.ActivationFunctionType
OP = mybir.AluOpType
AX = mybir.AxisListType.X
BF = ml_dtypes.bfloat16

NCORES = 8
DETECT_RACE = False
D = 2048
KC = D // 128
TPC = 2048
TT = 512
NTILES = TPC // TT
HID = 8192
NHEAD, HDIM, CHUNK = 16, 128, 256
ENGINES = ("sync", "vector", "scalar", "tensor", "gpsimd")


class _EngRec:

    def __init__(self, eng, ctx):
        self._eng = eng
        self._ctx = ctx

    def __getattr__(self, name):
        fn = getattr(self._eng, name)
        if not callable(fn):
            return fn
        ctx = self._ctx

        def wrap(*a, **kw):
            r = fn(*a, **kw)
            ctx.last = r
            return r
        return wrap


class _Ctx:
    def __init__(self, eng, dsem):
        self.raw = eng
        self.e = _EngRec(eng, self)
        self.dsem = dsem
        self.ndma = 0
        self.last = None

    def dma(self, out, in_, slow=False):
        self.raw.dma_start(out=out, in_=in_,
                           allow_slow_non_contiguous=slow).then_inc(
            self.dsem, 16)
        self.ndma += 1


class SP:

    def __init__(self, nc):
        self.nc = nc
        self.stages = []

    def stage(self, name=""):
        st = {e: [] for e in ENGINES}
        self.stages.append(st)

        class Rec:
            pass

        r = Rec()
        for e in ENGINES:
            def add(fn, e=e, st=st):
                st[e].append(fn)
                return fn
            setattr(r, e, add)
        return r

    def emit(self, ctx):
        nc = self.nc
        stg = ctx.enter_context(nc.semaphore("stg"))
        dsems = {e: ctx.enter_context(nc.semaphore(f"dsem_{e}"))
                 for e in ENGINES}
        parts = [[e for e in ENGINES if st[e]] for st in self.stages]
        thr = []
        tot = 0
        for p in parts:
            tot += len(p)
            thr.append(tot)
        block = ctx.enter_context(nc.Block())
        for ename in ENGINES:
            def body(eng, ename=ename):
                c = _Ctx(eng, dsems[ename])
                waited = 0
                for si, st in enumerate(self.stages):
                    fns = st[ename]
                    if not fns:
                        continue
                    need = thr[si - 1] if si > 0 else 0
                    if need > 0:
                        eng.wait_ge(stg, need)
                    c.last = None
                    for fn in fns:
                        fn(c)
                    if c.ndma > waited:
                        eng.wait_ge(dsems[ename], 16 * c.ndma)
                        waited = c.ndma
                        eng.nop().then_inc(stg, 1)
                    elif c.last is not None:
                        eng.drain().then_inc(stg, 1)
                    else:
                        eng.nop().then_inc(stg, 1)
            getattr(block, ename)(body)


def alloc_common(nc, ctx):
    t = {}
    t["ones_col_b"] = ctx.enter_context(
        nc.sbuf_tensor("ones_col_b", [128, 1], BF16))
    t["ones_row_f"] = ctx.enter_context(
        nc.sbuf_tensor("ones_row_f", [1, 128], F32))
    return t


def alloc_ln(nc, ctx, t, pref, n, ps8):
    sb = lambda nm, sh, dt: ctx.enter_context(nc.sbuf_tensor(nm, sh, dt))
    t[pref + "_xb"] = sb(pref + "_xb", [128, KC * n], BF16)
    t[pref + "_xsq"] = sb(pref + "_xsq", [128, KC * n], BF16)
    t[pref + "_ps_s"] = ps8[0][0:1, 0:n]
    t[pref + "_ps_q"] = ps8[1][0:1, 0:n]
    for nm in ("rm", "rq", "rt", "rr"):
        t[pref + "_" + nm] = sb(pref + "_" + nm, [1, n], F32)
    t[pref + "_mb"] = sb(pref + "_mb", [128, n], F32)
    t[pref + "_rb"] = sb(pref + "_rb", [128, n], F32)
    t[pref + "_ps_bm"] = ps8[2][:, 0:n]
    t[pref + "_ps_br"] = ps8[3][:, 0:n]


class WPool:

    def __init__(self, nc, ctx, tag, n):
        self.slots = [ctx.enter_context(
            nc.sbuf_tensor(f"{tag}_ws{i}", [128, KC * 512], BF16))
            for i in range(n)]
        self.n = n
        self.load_i = 0
        self.use_i = 0

    def load_fn(self, dram_ap, ncols=KC * 512):
        i = self.load_i
        assert i - self.use_i < self.n, "WPool overrun"
        self.load_i += 1
        slot = self.slots[i % self.n]

        def go(c, slot=slot, ap=dram_ap, ncols=ncols):
            c.dma(slot[:, 0:ncols].rearrange("p (kc o) -> p kc o", o=512), ap)
        return go

    def use(self):
        i = self.use_i
        assert i < self.load_i, "WPool use before load"
        self.use_i += 1
        return self.slots[i % self.n]


def emit_ln_stats(sp, nc, t, xin_chunks, n, pref):
    xb, xsq = t[pref + "_xb"], t[pref + "_xsq"]
    ps_s, ps_q = t[pref + "_ps_s"], t[pref + "_ps_q"]
    rm, rq, rt, rr = (t[pref + "_rm"], t[pref + "_rq"],
                      t[pref + "_rt"], t[pref + "_rr"])
    s = sp.stage(pref + "_cast")
    for i in range(KC):
        s.vector(lambda c, i=i: c.e.tensor_copy(
            xb[:, i * n:(i + 1) * n], xin_chunks[i]))
        s.scalar(lambda c, i=i: c.e.activation(
            xsq[:, i * n:(i + 1) * n], xin_chunks[i], AF.Square))
    s = sp.stage(pref + "_summ")
    def mm_sums(c):
        for i in range(KC):
            c.e.matmul(ps_s, t["ones_col_b"][:], xb[:, i * n:(i + 1) * n],
                       start=(i == 0), stop=(i == KC - 1))
        for i in range(KC):
            c.e.matmul(ps_q, t["ones_col_b"][:], xsq[:, i * n:(i + 1) * n],
                       start=(i == 0), stop=(i == KC - 1))
    s.tensor(mm_sums)
    s = sp.stage(pref + "_rows")
    def rows(c):
        v = c.e
        v.tensor_scalar_mul(rm[:], ps_s, 1.0 / D)
        v.tensor_scalar_mul(rq[:], ps_q, 1.0 / D)
        v.drain()
        v.tensor_mul(rt[:], rm[:], rm[:])
        v.drain()
        v.tensor_sub(rq[:], rq[:], rt[:])
        v.drain()
        v.tensor_scalar_add(rq[:], rq[:], 1e-5)
    s.vector(rows)
    s = sp.stage(pref + "_rstd")
    def rstd(c):
        sc = c.e
        sc.activation(rt[:], rq[:], AF.Ln)
        sc.drain()
        sc.activation(rr[:], rt[:], AF.Exp, scale=-0.5)
    s.scalar(rstd)
    s = sp.stage(pref + "_bcast")
    def bc(c):
        c.e.matmul(t[pref + "_ps_bm"], t["ones_row_f"][:], rm[:],
                   start=True, stop=True)
        c.e.matmul(t[pref + "_ps_br"], t["ones_row_f"][:], rr[:],
                   start=True, stop=True)
    s.tensor(bc)
    s = sp.stage(pref + "_bev")
    s.vector(lambda c: c.e.tensor_copy(t[pref + "_mb"][:],
                                       t[pref + "_ps_bm"]))
    s.vector(lambda c: c.e.tensor_copy(t[pref + "_rb"][:],
                                       t[pref + "_ps_br"]))


def emit_ln_apply(sp, name, xin, out, mb, rb, scr4, n):
    s = sp.stage(name)
    def go(c):
        v = c.e
        for r in range(4):
            for j in range(4):
                i = r * 4 + j
                v.tensor_sub(scr4[:, j * n:(j + 1) * n], xin[i], mb[:])
            v.drain()
            for j in range(4):
                i = r * 4 + j
                v.tensor_mul(out[i], scr4[:, j * n:(j + 1) * n], rb[:])
            v.drain()
    s.vector(go)


def emit_res_add(sp, name, x_chunks, ln_chunks):
    s = sp.stage(name)
    def go(c):
        v = c.e
        for i in range(KC):
            v.tensor_add(x_chunks[i], x_chunks[i], ln_chunks[i])
    s.vector(go)


def phase_a(nc, sp, t, io):
    xT, prev, wk, wv, wr, tm = (io["xT"], io["prev"], io["wkT"], io["wvT"],
                                io["wrT"], io["tm"])
    ko, vo, so = io["k_d"], io["v_d"], io["sr_d"]
    with ExitStack() as ctx:
        sb = lambda nm, sh, dt=F32: ctx.enter_context(nc.sbuf_tensor(nm, sh, dt))
        psf = lambda nm, sh: ctx.enter_context(nc.psum_tensor(nm, sh, F32))
        xt = sb("a_xt", [128, KC * TT])
        xln = sb("a_xln", [128, KC * TT], BF16)
        lxcol = sb("a_lxcol", [128, KC], BF16)
        prevt = sb("a_prevt", [128, KC])
        prevb = sb("a_prevb", [128, KC], BF16)
        prevq = sb("a_prevq", [128, KC], BF16)
        tmt = sb("a_tmt", [128, KC * 3])
        dmx = sb("a_dmx", [128, 4 * TT], BF16)
        mix = sb("a_mix", [128, 3 * KC * TT], BF16)
        evac = [sb(f"a_evac{i}", [128, 4 * TT]) for i in range(2)]
        ps8 = [psf(f"a_b{i}", [128, TT]) for i in range(8)]
        alloc_ln(nc, ctx, t, "aln", TT, ps8)
        pool = WPool(nc, ctx, "a", 2)
        wdram = [wk, wv, wr]
        odram = [ko, vo, so]

        def wap(lu):
            pj, ob = lu // 4, lu % 4
            return wdram[pj][:, ob * 512:(ob + 1) * 512].rearrange(
                "(kc p) o -> p kc o", p=128)

        s = sp.stage("tm_load")
        s.gpsimd(pool.load_fn(wap(0)))
        s.sync(lambda c: c.dma(
            tmt[:].rearrange("p (kc three) -> p kc three", three=3),
            tm.rearrange("(kc p) three -> p kc three", p=128), slow=True))
        s.sync(lambda c: c.dma(
            prevt[:].rearrange("p (kc one) -> p kc one", one=1),
            prev.rearrange("(kc p) one -> p kc one", p=128), slow=True))
        s = sp.stage("pv_cast")
        s.vector(lambda c: c.e.tensor_copy(prevb[:], prevt[:]))
        s.scalar(lambda c: c.e.activation(prevq[:], prevt[:], AF.Square))
        s = sp.stage("pv_summ")
        def pv_mm(c):
            c.e.matmul(t["aln_ps_s"][:, 0:KC], t["ones_col_b"][:], prevb[:],
                       start=True, stop=True)
            c.e.matmul(t["aln_ps_q"][:, 0:KC], t["ones_col_b"][:], prevq[:],
                       start=True, stop=True)
        s.tensor(pv_mm)
        s = sp.stage("pv_rows")
        def pv_rows(c):
            v = c.e
            v.reduce_sum(t["aln_rm"][:, 0:1], t["aln_ps_s"][:, 0:KC], axis=AX)
            v.reduce_sum(t["aln_rq"][:, 0:1], t["aln_ps_q"][:, 0:KC], axis=AX)
            v.drain()
            v.tensor_scalar_mul(t["aln_rm"][:, 0:1], t["aln_rm"][:, 0:1], 1.0 / D)
            v.tensor_scalar_mul(t["aln_rq"][:, 0:1], t["aln_rq"][:, 0:1], 1.0 / D)
            v.drain()
            v.tensor_mul(t["aln_rt"][:, 0:1], t["aln_rm"][:, 0:1],
                         t["aln_rm"][:, 0:1])
            v.drain()
            v.tensor_sub(t["aln_rq"][:, 0:1], t["aln_rq"][:, 0:1],
                         t["aln_rt"][:, 0:1])
            v.drain()
            v.tensor_scalar_add(t["aln_rq"][:, 0:1], t["aln_rq"][:, 0:1], 1e-5)
        s.vector(pv_rows)
        s = sp.stage("pv_rstd")
        def pv_rstd(c):
            sc = c.e
            sc.activation(t["aln_rt"][:, 0:1], t["aln_rq"][:, 0:1], AF.Ln)
            sc.drain()
            sc.activation(t["aln_rr"][:, 0:1], t["aln_rt"][:, 0:1], AF.Exp,
                          scale=-0.5)
        s.scalar(pv_rstd)
        s = sp.stage("pv_bc")
        def pv_bc(c):
            c.e.matmul(t["aln_ps_bm"][:, 0:1], t["ones_row_f"][:],
                       t["aln_rm"][:, 0:1], start=True, stop=True)
            c.e.matmul(t["aln_ps_br"][:, 0:1], t["ones_row_f"][:],
                       t["aln_rr"][:, 0:1], start=True, stop=True)
        s.tensor(pv_bc)
        s = sp.stage("pv_apply")
        def pv_ap(c):
            v = c.e
            v.tensor_copy(t["aln_mb"][:, 0:1], t["aln_ps_bm"][:, 0:1])
            v.tensor_copy(t["aln_rb"][:, 0:1], t["aln_ps_br"][:, 0:1])
            v.drain()
            v.tensor_scalar(lxcol[:], prevt[:], t["aln_mb"][:, 0:1],
                            t["aln_rb"][:, 0:1], OP.subtract, OP.mult)
        s.vector(pv_ap)

        for tile in range(NTILES):
            off = tile * TT
            s = sp.stage(f"x_load{tile}")
            s.sync(lambda c, off=off: c.dma(
                xt[:].rearrange("p (kc n) -> p kc n", kc=KC),
                xT[:, off:off + TT].rearrange("(kc p) n -> p kc n", p=128)))
            xin = [xt[:, i * TT:(i + 1) * TT] for i in range(KC)]
            emit_ln_stats(sp, nc, t, xin, TT, "aln")
            emit_ln_apply(sp, f"a_xln{tile}", xin,
                          [xln[:, i * TT:(i + 1) * TT] for i in range(KC)],
                          t["aln_mb"], t["aln_rb"], evac[0], TT)
            s = sp.stage(f"a_mix{tile}")
            def mx_all(c):
                v = c.e
                for r in range(4):
                    for j in range(4):
                        i = r * 4 + j
                        base = i * TT
                        v.tensor_sub(dmx[:, j * TT + 1:(j + 1) * TT],
                                     xln[:, base + 1:base + TT],
                                     xln[:, base:base + TT - 1])
                        v.tensor_sub(dmx[:, j * TT:j * TT + 1],
                                     xln[:, base:base + 1], lxcol[:, i:i + 1])
                    v.drain()
                    for j in range(4):
                        i = r * 4 + j
                        base = i * TT
                        lx1 = xln[:, base:base + TT - 1]
                        lx0 = lxcol[:, i:i + 1]
                        for jj in range(3):
                            mslot = mix[:, (jj * KC + i) * TT:
                                        (jj * KC + i + 1) * TT]
                            v.scalar_tensor_tensor(
                                mslot[:, 1:TT], dmx[:, j * TT + 1:(j + 1) * TT],
                                tmt[:, i * 3 + jj:i * 3 + jj + 1], lx1,
                                OP.mult, OP.add)
                            v.scalar_tensor_tensor(
                                mslot[:, 0:1], dmx[:, j * TT:j * TT + 1],
                                tmt[:, i * 3 + jj:i * 3 + jj + 1], lx0,
                                OP.mult, OP.add)
                    v.drain()
            s.vector(mx_all)
            s = sp.stage(f"a_lxsave{tile}")
            def lxs(c):
                v = c.e
                for i in range(KC):
                    v.tensor_copy(lxcol[:, i:i + 1],
                                  xln[:, i * TT + TT - 1:i * TT + TT])
            s.vector(lxs)
            def mk_ev(lu):
                pj = lu // 4
                def ev(s, lu=lu, pj=pj):
                    for oc in range(4):
                        if pj == 2:
                            s.scalar(lambda c, oc=oc, lu=lu: c.e.activation(
                                evac[lu % 2][:, oc * TT:(oc + 1) * TT],
                                ps8[(lu % 2) * 4 + oc][:], AF.Sigmoid))
                        else:
                            s.vector(lambda c, oc=oc, lu=lu: c.e.tensor_copy(
                                evac[lu % 2][:, oc * TT:(oc + 1) * TT],
                                ps8[(lu % 2) * 4 + oc][:]))
                return ev

            def mk_st(lu):
                pj, ob = lu // 4, lu % 4
                def st(s, lu=lu, pj=pj, ob=ob):
                    s.sync(lambda c, pj=pj, ob=ob, off=off, lu=lu: c.dma(
                        odram[pj][ob * 512:(ob + 1) * 512, off:off + TT]
                        .rearrange("(oc p) n -> p oc n", p=128),
                        evac[lu % 2][:].rearrange("p (oc n) -> p oc n", oc=4)))
                return st

            for lu in range(12):
                s = sp.stage(f"a_u{tile}_{lu}")
                if tile * 12 + lu + 1 < NTILES * 12:
                    s.gpsimd(pool.load_fn(wap((lu + 1) % 12)))
                slot = pool.use()
                pj = lu // 4
                def mms(c, pj=pj, slot=slot, lu=lu):
                    for oc in range(4):
                        for k in range(KC):
                            c.e.matmul(
                                ps8[(lu % 2) * 4 + oc][:],
                                slot[:, k * 512 + oc * 128:
                                     k * 512 + (oc + 1) * 128],
                                mix[:, (pj * KC + k) * TT:
                                    (pj * KC + k + 1) * TT],
                                start=(k == 0), stop=(k == KC - 1))
                s.tensor(mms)
                if lu >= 1:
                    mk_ev(lu - 1)(s)
                if lu >= 2:
                    mk_st(lu - 2)(s)
            s = sp.stage(f"a_fl1_{tile}")
            mk_ev(11)(s)
            mk_st(10)(s)
            s = sp.stage(f"a_fl2_{tile}")
            mk_st(11)(s)


def phase_scan(nc, sp, t, io, tag, full, st):
    G = 8
    kd, vd = io["k_d"], io["v_d"]
    td, tf = io["td"], io["tf"]
    wo = io["wkv_d"]
    NT = NTILES
    with ExitStack() as ctx:
        sb = lambda nm, sh, dt=F32: ctx.enter_context(nc.sbuf_tensor(nm, sh, dt))
        kt = [sb(f"{tag}_kt{i}", [128, G * TT]) for i in range(2)]
        vt = [sb(f"{tag}_vt{i}", [128, G * TT]) for i in range(2)]
        ekm = sb(tag + "_ekm", [128, G * TT])
        pt = sb(tag + "_pt", [128, G * TT])
        abuf = sb(tag + "_abuf", [128, G * (TT + 1)])
        bbuf = sb(tag + "_bbuf", [128, G * (TT + 1)])
        dect = sb(tag + "_dect", [128, G * TT])
        onesf = sb(tag + "_onesf", [128, TT])
        cols = sb(tag + "_cols", [128, 8 * G])
        Mprev = sb(tag + "_Mprev", [128, G])
        einit = sb(tag + "_einit", [128, G])
        if full:
            numt = sb(tag + "_numt", [128, G * TT])
            dent = sb(tag + "_dent", [128, G * TT])
            wkvt = sb(tag + "_wkvt", [128, G * TT])

        def emit_ld(s, g):
            ch, ti = g // NT, g % NT
            r0, off = ch * 1024, ti * TT
            s.sync(lambda c, r0=r0, off=off, g=g: c.dma(
                kt[g % 2][:].rearrange("p (g n) -> p g n", g=G),
                kd[r0:r0 + 1024, off:off + TT]
                .rearrange("(g p) n -> p g n", p=128)))
            s.sync(lambda c, r0=r0, off=off, g=g: c.dma(
                vt[g % 2][:].rearrange("p (g n) -> p g n", g=G),
                vd[r0:r0 + 1024, off:off + TT]
                .rearrange("(g p) n -> p g n", p=128)))

        def emit_stout(s, g):
            ch, ti = g // NT, g % NT
            r0, off = ch * 1024, ti * TT
            s.sync(lambda c, r0=r0, off=off: c.dma(
                wo[r0:r0 + 1024, off:off + TT]
                .rearrange("(g p) n -> p g n", p=128),
                wkvt[:].rearrange("p (g n) -> p g n", g=G)))

        for ch in range(2):
            r0 = ch * 1024
            s = sp.stage(f"{tag}_cols{ch}")
            s.sync(lambda c, r0=r0: c.dma(
                cols[:, 0:G].rearrange("p (g one) -> p g one", one=1),
                td[r0:r0 + 1024].rearrange("(g p) one -> p g one", p=128),
                slow=True))
            s.sync(lambda c, r0=r0: c.dma(
                cols[:, G:2 * G].rearrange("p (g one) -> p g one", one=1),
                tf[r0:r0 + 1024].rearrange("(g p) one -> p g one", p=128),
                slow=True))
            if ch == 0:
                emit_ld(s, 0)
            s = sp.stage(f"{tag}_prep{ch}")
            def prep_s(c):
                sc = c.e
                sc.activation(cols[:, 2 * G:3 * G], cols[:, 0:G], AF.Exp)
                sc.activation(cols[:, 4 * G:5 * G], cols[:, G:2 * G], AF.Exp)
            s.scalar(prep_s)
            s = sp.stage(f"{tag}_prep2{ch}")
            s.scalar(lambda c: c.e.activation(cols[:, 3 * G:4 * G],
                                              cols[:, 2 * G:3 * G], AF.Exp,
                                              scale=-1.0))
            s = sp.stage(f"{tag}_prep3{ch}")
            def prep_v(c):
                v = c.e
                v.memset(onesf[:], 1.0)
                v.drain()
                for g in range(G):
                    v.tensor_scalar_mul(dect[:, g * TT:(g + 1) * TT], onesf[:],
                                        cols[:, 3 * G + g:3 * G + g + 1])
            s.vector(prep_v)

            for ti in range(NT):
                gidx = ch * NT + ti
                ktc, vtc = kt[gidx % 2], vt[gidx % 2]
                s = sp.stage(f"{tag}_s1_{ch}_{ti}")
                if gidx + 1 < 2 * NT:
                    emit_ld(s, gidx + 1)
                def mx(c, ti=ti, ch=ch, ktc=ktc):
                    v = c.e
                    v.reduce_max(cols[:, 5 * G:6 * G],
                                 ktc[:].rearrange("p (g n) -> p g n", g=G),
                                 axis=AX)
                    v.drain()
                    v.tensor_scalar_mul(cols[:, 6 * G:7 * G],
                                        cols[:, 5 * G:6 * G], -1.0)
                    if ti > 0:
                        v.tensor_sub(cols[:, 7 * G:8 * G], Mprev[:],
                                     cols[:, 5 * G:6 * G])
                        v.drain()
                    elif full:
                        v.tensor_sub(einit[:], st["stE"][:, ch * G:(ch + 1) * G],
                                     cols[:, 5 * G:6 * G])
                        v.drain()
                    v.tensor_copy(Mprev[:], cols[:, 5 * G:6 * G])
                s.vector(mx)
                if full and gidx >= 1:
                    emit_stout(s, gidx - 1)
                s = sp.stage(f"{tag}_s2_{ch}_{ti}")
                def ex(c, ti=ti, ktc=ktc):
                    sc = c.e
                    for g in range(G):
                        sc.activation(ekm[:, g * TT:(g + 1) * TT],
                                      ktc[:, g * TT:(g + 1) * TT], AF.Exp,
                                      bias=cols[:, 6 * G + g:6 * G + g + 1])
                    if ti > 0:
                        sc.activation(cols[:, 7 * G:8 * G], cols[:, 7 * G:8 * G],
                                      AF.Exp)
                    elif full:
                        sc.activation(einit[:], einit[:], AF.Exp)
                s.scalar(ex)
                s = sp.stage(f"{tag}_s3_{ch}_{ti}")
                def scan(c, ti=ti, ch=ch, vtc=vtc):
                    v = c.e
                    v.tensor_mul(pt[:], ekm[:], vtc[:])
                    for g in range(G):
                        a0 = abuf[:, g * (TT + 1):g * (TT + 1) + 1]
                        b0 = bbuf[:, g * (TT + 1):g * (TT + 1) + 1]
                        if ti == 0:
                            if full:
                                cg = ch * G + g
                                v.tensor_mul(a0, st["stAm"][:, cg:cg + 1],
                                             einit[:, g:g + 1])
                                v.tensor_mul(b0, st["stBm"][:, cg:cg + 1],
                                             einit[:, g:g + 1])
                            else:
                                v.memset(a0, 0.0)
                                v.memset(b0, 0.0)
                        else:
                            v.tensor_mul(a0, abuf[:, g * (TT + 1) + TT:
                                                  g * (TT + 1) + TT + 1],
                                         cols[:, 7 * G + g:7 * G + g + 1])
                            v.tensor_mul(b0, bbuf[:, g * (TT + 1) + TT:
                                                  g * (TT + 1) + TT + 1],
                                         cols[:, 7 * G + g:7 * G + g + 1])
                    v.drain()
                    for g in range(G):
                        sl = slice(g * (TT + 1) + 1, g * (TT + 1) + TT + 1)
                        v.tensor_tensor_scan(
                            abuf[:, sl], dect[:, g * TT:(g + 1) * TT],
                            pt[:, g * TT:(g + 1) * TT],
                            abuf[:, g * (TT + 1):g * (TT + 1) + 1],
                            OP.mult, OP.add)
                        v.tensor_tensor_scan(
                            bbuf[:, sl], dect[:, g * TT:(g + 1) * TT],
                            ekm[:, g * TT:(g + 1) * TT],
                            bbuf[:, g * (TT + 1):g * (TT + 1) + 1],
                            OP.mult, OP.add)
                    v.drain()
                    if full:
                        for g in range(G):
                            pv = slice(g * (TT + 1), g * (TT + 1) + TT)
                            v.scalar_tensor_tensor(
                                numt[:, g * TT:(g + 1) * TT],
                                pt[:, g * TT:(g + 1) * TT],
                                cols[:, 4 * G + g:4 * G + g + 1], abuf[:, pv],
                                OP.mult, OP.add)
                            v.scalar_tensor_tensor(
                                dent[:, g * TT:(g + 1) * TT],
                                ekm[:, g * TT:(g + 1) * TT],
                                cols[:, 4 * G + g:4 * G + g + 1], bbuf[:, pv],
                                OP.mult, OP.add)
                s.vector(scan)
                if full:
                    s = sp.stage(f"{tag}_s4_{ch}_{ti}")
                    def dv(c):
                        sc = c.e
                        sc.activation(dent[:], dent[:], AF.Ln)
                        sc.drain()
                        sc.activation(dent[:], dent[:], AF.Exp, scale=-1.0)
                    s.scalar(dv)
                    s = sp.stage(f"{tag}_s5_{ch}_{ti}")
                    s.vector(lambda c: c.e.tensor_mul(wkvt[:], numt[:],
                                                      dent[:]))
            if not full:
                s = sp.stage(f"{tag}_stx{ch}")
                def stx(c, ch=ch):
                    v = c.e
                    for g in range(G):
                        cg = ch * G + g
                        v.tensor_copy(st["st0loc"][:, cg:cg + 1],
                                      abuf[:, g * (TT + 1) + TT:
                                           g * (TT + 1) + TT + 1])
                        v.tensor_copy(st["st0loc"][:, 16 + cg:17 + cg],
                                      bbuf[:, g * (TT + 1) + TT:
                                           g * (TT + 1) + TT + 1])
                    v.tensor_copy(st["st0loc"][:, 32 + ch * G:32 + (ch + 1) * G],
                                  Mprev[:])
                s.vector(stx)
        if full:
            s = sp.stage(f"{tag}_stoutF")
            emit_stout(s, 2 * NT - 1)


def emit_exchange(nc, sp, tag, V, pub_sb, rem_sb, sh_data, sh_flag, flag_col,
                  ctrlt, flagt, spinsem):
    nonce_col = 2 + flag_col
    s = sp.stage(tag + "_pub")
    s.gpsimd(lambda c: c.dma(sh_data[ts(V["my"], 128), :], pub_sb[:]))
    s = sp.stage(tag + "_flag")
    s.gpsimd(lambda c: c.dma(sh_flag[ts(V["my"], 1),
                             flag_col:flag_col + 1],
                             ctrlt[0:1, nonce_col:nonce_col + 1]))
    s = sp.stage(tag + "_spin")
    def spin(c):
        g = c.raw
        engs = OrderedEngineSet([g.engine])
        regs = nc.alloc_registers(tag + "_sr", engines=engs)
        nc.regs_mov(regs, 0)
        g.br(tag + "_loop")
        with nc.body(tag + "_loop", valid_engines=engs):
            g.dma_start(out=flagt[:, 0:16],
                        in_=sh_flag[ts(V["pr"], 1), :]).then_inc(spinsem, 16)
            for _ in range(64):
                g.nop()
            g.reg_load(regs[g.engine], flagt[0:1, flag_col:flag_col + 1])
            nc.br_cmp(regs, V["nonce"], tag + "_done", tag + "_loop",
                      "IS_EQ", engines=engs)
        nc.switch_bb(tag + "_done")
        c.dma(rem_sb[:], sh_data[ts(V["pr"], 128), :])
    s.gpsimd(spin)


def phase_c(nc, sp, t, io):
    xT, wkvT, srT = io["xT"], io["wkv_d"], io["sr_d"]
    woT, wgT, wuT, wdT = io["woT"], io["wgT"], io["wuT"], io["wdT"]
    x2o = io["x2_d"]
    with ExitStack() as ctx:
        sb = lambda nm, sh, dt=F32: ctx.enter_context(nc.sbuf_tensor(nm, sh, dt))
        psf = lambda nm, sh: ctx.enter_context(nc.psum_tensor(nm, sh, F32))
        xt = sb("c_xt", [128, KC * TT])
        ao = sb("c_ao", [128, KC * TT])
        zt = sb("c_zt", [128, KC * TT], BF16)
        scr = sb("c_scr", [128, 4 * TT])
        act_q = sb("c_act_q", [128, 16 * TT], BF16)
        evac = sb("c_evac", [128, 4 * TT])
        ps8 = [psf(f"c_b{i}", [128, TT]) for i in range(8)]
        alloc_ln(nc, ctx, t, "cln", TT, ps8)
        pool = WPool(nc, ctx, "c", 3)
        psG = ps8[0:4]
        psU = ps8[4:8]

        def tile_aps():
            aps = []
            for ob in range(4):
                aps.append(woT[:, ob * 512:(ob + 1) * 512].rearrange(
                    "(kc p) o -> p kc o", p=128))
            for q in range(4):
                for hb in range(4):
                    h0 = q * 2048 + hb * 512
                    aps.append(wgT[:, h0:h0 + 512].rearrange(
                        "(kc p) o -> p kc o", p=128))
                    aps.append(wuT[:, h0:h0 + 512].rearrange(
                        "(kc p) o -> p kc o", p=128))
                for db in range(4):
                    aps.append(wdT[q * 2048:(q + 1) * 2048,
                                   db * 512:(db + 1) * 512].rearrange(
                        "(hc p) o -> p hc o", p=128))
            return aps

        all_aps = []
        for tile in range(NTILES):
            all_aps.extend(tile_aps())
        pi = [0]

        def pf(s):
            if pi[0] < len(all_aps) and pool.load_i - pool.use_i < 3:
                s.gpsimd(pool.load_fn(all_aps[pi[0]]))
                pi[0] += 1

        def mm16(c, slot, psd, rhs, nk=KC):
            for oc in range(4):
                for k in range(nk):
                    c.e.matmul(
                        psd[oc][:],
                        slot[:, k * 512 + oc * 128:k * 512 + (oc + 1) * 128],
                        rhs[:, k * TT:(k + 1) * TT],
                        start=(k == 0), stop=(k == nk - 1))

        for tile in range(NTILES):
            off = tile * TT
            s = sp.stage(f"c_zld{tile}")
            s.sync(lambda c, off=off: c.dma(
                xt[:].rearrange("p (kc n) -> p kc n", kc=KC),
                wkvT[:, off:off + TT].rearrange("(kc p) n -> p kc n", p=128)))
            s.sync(lambda c, off=off: c.dma(
                ao[:].rearrange("p (kc n) -> p kc n", kc=KC),
                srT[:, off:off + TT].rearrange("(kc p) n -> p kc n", p=128)))
            pf(s)
            s = sp.stage(f"c_zmul{tile}")
            s.vector(lambda c: c.e.tensor_mul(zt[:], xt[:], ao[:]))
            pf(s)
            for ob in range(4):
                s = sp.stage(f"c_wo{tile}_{ob}")
                pf(s)
                slot = pool.use()
                s.tensor(lambda c, slot=slot, ob=ob: mm16(
                    c, slot, psG if ob % 2 == 0 else psU, zt))
                if ob >= 1:
                    def ev(s, ob=ob):
                        psd = psG if (ob - 1) % 2 == 0 else psU
                        for oc in range(4):
                            s.vector(lambda c, ob=ob, oc=oc, psd=psd:
                                     c.e.tensor_copy(
                                ao[:, ((ob - 1) * 4 + oc) * TT:
                                   ((ob - 1) * 4 + oc + 1) * TT], psd[oc][:]))
                    ev(s)
            s = sp.stage(f"c_woev3_{tile}")
            for oc in range(4):
                s.vector(lambda c, oc=oc: c.e.tensor_copy(
                    ao[:, (3 * 4 + oc) * TT:(3 * 4 + oc + 1) * TT],
                    psU[oc][:]))
            pf(s)
            aoc = [ao[:, i * TT:(i + 1) * TT] for i in range(KC)]
            emit_ln_stats(sp, nc, t, aoc, TT, "cln")
            s = sp.stage(f"c_xld{tile}")
            s.sync(lambda c, off=off: c.dma(
                xt[:].rearrange("p (kc n) -> p kc n", kc=KC),
                xT[:, off:off + TT].rearrange("(kc p) n -> p kc n", p=128)))
            pf(s)
            aoc_l = [ao[:, i * TT:(i + 1) * TT] for i in range(KC)]
            emit_ln_apply(sp, f"c_lna1_{tile}", aoc_l, aoc_l,
                          t["cln_mb"], t["cln_rb"], scr, TT)
            emit_res_add(sp, f"c_res1_{tile}",
                         [xt[:, i * TT:(i + 1) * TT] for i in range(KC)],
                         aoc_l)
            xc = [xt[:, i * TT:(i + 1) * TT] for i in range(KC)]
            emit_ln_stats(sp, nc, t, xc, TT, "cln")
            emit_ln_apply(sp, f"c_xln2_{tile}", xc,
                          [zt[:, i * TT:(i + 1) * TT] for i in range(KC)],
                          t["cln_mb"], t["cln_rb"], scr, TT)
            for q in range(4):
                for hb in range(4):
                    s = sp.stage(f"c_fg{tile}_{q}_{hb}")
                    pf(s)
                    slot = pool.use()
                    s.tensor(lambda c, slot=slot: mm16(c, slot, psG, zt))
                    if hb >= 1:
                        def fact(s, hb=hb):
                            for oc in range(4):
                                s.vector(lambda c, hb=hb, oc=oc:
                                         c.e.tensor_mul(
                                    act_q[:, ((hb - 1) * 4 + oc) * TT:
                                          ((hb - 1) * 4 + oc + 1) * TT],
                                    evac[:, oc * TT:(oc + 1) * TT],
                                    psU[oc][:]))
                        fact(s)
                    s = sp.stage(f"c_fu{tile}_{q}_{hb}")
                    pf(s)
                    slot = pool.use()
                    s.tensor(lambda c, slot=slot: mm16(c, slot, psU, zt))
                    for oc in range(4):
                        s.scalar(lambda c, oc=oc: c.e.activation(
                            evac[:, oc * TT:(oc + 1) * TT], psG[oc][:],
                            AF.Silu))
                s = sp.stage(f"c_fact3_{tile}_{q}")
                pf(s)
                for oc in range(4):
                    s.vector(lambda c, oc=oc: c.e.tensor_mul(
                        act_q[:, (3 * 4 + oc) * TT:(3 * 4 + oc + 1) * TT],
                        evac[:, oc * TT:(oc + 1) * TT], psU[oc][:]))
                for db in range(4):
                    s = sp.stage(f"c_fdm{tile}_{q}_{db}")
                    pf(s)
                    slot = pool.use()
                    s.tensor(lambda c, slot=slot, db=db: mm16(
                        c, slot, psG if db % 2 == 0 else psU, act_q, nk=16))
                    if db >= 1:
                        def fde(s, db=db, q=q):
                            psd = psG if (db - 1) % 2 == 0 else psU
                            for oc in range(4):
                                ss = slice(((db - 1) * 4 + oc) * TT,
                                           ((db - 1) * 4 + oc + 1) * TT)
                                if q == 0:
                                    s.vector(lambda c, ss=ss, oc=oc, psd=psd:
                                             c.e.tensor_copy(
                                        ao[:, ss], psd[oc][:]))
                                else:
                                    s.vector(lambda c, ss=ss, oc=oc, psd=psd:
                                             c.e.tensor_add(
                                        ao[:, ss], ao[:, ss], psd[oc][:]))
                        fde(s)
                s = sp.stage(f"c_fde3_{tile}_{q}")
                pf(s)
                for oc in range(4):
                    ss = slice((3 * 4 + oc) * TT, (3 * 4 + oc + 1) * TT)
                    if q == 0:
                        s.vector(lambda c, ss=ss, oc=oc: c.e.tensor_copy(
                            ao[:, ss], psU[oc][:]))
                    else:
                        s.vector(lambda c, ss=ss, oc=oc: c.e.tensor_add(
                            ao[:, ss], ao[:, ss], psU[oc][:]))
            fc = [ao[:, i * TT:(i + 1) * TT] for i in range(KC)]
            emit_ln_stats(sp, nc, t, fc, TT, "cln")
            emit_ln_apply(sp, f"c_lna2_{tile}", fc, fc,
                          t["cln_mb"], t["cln_rb"], scr, TT)
            emit_res_add(sp, f"c_res2_{tile}",
                         [xt[:, i * TT:(i + 1) * TT] for i in range(KC)], fc)
            s = sp.stage(f"c_x2st{tile}")
            s.sync(lambda c, off=off: c.dma(
                x2o[:, off:off + TT].rearrange("(kc p) n -> p kc n", p=128),
                xt[:].rearrange("p (kc n) -> p kc n", kc=KC)))
            pf(s)
        assert pool.load_i == pool.use_i == len(all_aps), (
            pool.load_i, pool.use_i, len(all_aps))


def phase_d(nc, sp, t, io):
    xT = io["x2_d"]
    wq, cosd, sind, rotd = io["wqkvT"], io["cosd"], io["sind"], io["rotP"]
    qo, ko, vo = io["q_d"], io["k1_d"], io["v1_d"]
    with ExitStack() as ctx:
        sb = lambda nm, sh, dt=F32: ctx.enter_context(nc.sbuf_tensor(nm, sh, dt))
        psf = lambda nm, sh, dt=F32: ctx.enter_context(
            nc.psum_tensor(nm, sh, dt))
        xt = sb("d_xt", [128, KC * TT])
        xln = sb("d_xln", [128, KC * TT], BF16)
        scr = sb("d_scr", [128, 4 * TT])
        emin = [sb(f"d_emin{i}", [128, 4 * TT]) for i in range(2)]
        eexp = [sb(f"d_eexp{i}", [128, 4 * TT]) for i in range(2)]
        ebuf = [sb(f"d_ebuf{i}", [128, 4 * TT]) for i in range(2)]
        qkb = [sb(f"d_qkb{i}", [128, 4 * TT], BF16) for i in range(2)]
        outb = [sb(f"d_outb{i}", [128, 4 * TT], BF16) for i in range(2)]
        cost = sb("d_cost", [128, TT])
        sint = sb("d_sint", [128, TT])
        rott = sb("d_rott", [128, 128], BF16)
        ps8 = [psf(f"d_b{i}", [128, TT]) for i in range(8)]
        psM = ps8[0:4]
        psR = ps8[4:8]
        alloc_ln(nc, ctx, t, "dln", TT, ps8)
        pool = WPool(nc, ctx, "d", 2)

        def wap(ob):
            return wq[:, ob * 512:(ob + 1) * 512].rearrange(
                "(kc p) o -> p kc o", p=128)

        s = sp.stage("d_rot_load")
        s.gpsimd(lambda c: c.dma(rott[:], rotd))
        s.gpsimd(pool.load_fn(wap(0)))

        def emit_rope(s, u):
            p = u % 2
            def rope(c, p=p):
                v = c.e
                for pair in range(2):
                    o0, o1 = 2 * pair, 2 * pair + 1
                    s0 = slice(o0 * TT, (o0 + 1) * TT)
                    s1 = slice(o1 * TT, (o1 + 1) * TT)
                    v.tensor_mul(scr[:, 0:TT], ebuf[p][:, s0], cost[:])
                    v.tensor_mul(scr[:, TT:2 * TT], psR[o0][:], sint[:])
                    v.tensor_mul(scr[:, 2 * TT:3 * TT], ebuf[p][:, s1],
                                 cost[:])
                    v.tensor_mul(scr[:, 3 * TT:4 * TT], psR[o1][:], sint[:])
                    v.drain()
                    v.tensor_add(ebuf[p][:, s0], scr[:, 0:TT],
                                 scr[:, TT:2 * TT])
                    v.tensor_add(ebuf[p][:, s1], scr[:, 2 * TT:3 * TT],
                                 scr[:, 3 * TT:4 * TT])
                    v.drain()
                    v.tensor_scalar_min(emin[p][:, s0], ebuf[p][:, s0], 0.0)
                    v.tensor_scalar_max(ebuf[p][:, s0], ebuf[p][:, s0], 0.0)
                    v.tensor_scalar_min(emin[p][:, s1], ebuf[p][:, s1], 0.0)
                    v.tensor_scalar_max(ebuf[p][:, s1], ebuf[p][:, s1], 0.0)
            s.vector(rope)

        def emit_elu(s, u):
            p = u % 2
            for oc in range(4):
                s.scalar(lambda c, oc=oc, p=p: c.e.activation(
                    eexp[p][:, oc * TT:(oc + 1) * TT],
                    emin[p][:, oc * TT:(oc + 1) * TT], AF.Exp))

        def emit_elu2(s, u):
            p = u % 2
            for oc in range(4):
                s.vector(lambda c, oc=oc, p=p: c.e.tensor_add(
                    outb[p][:, oc * TT:(oc + 1) * TT],
                    ebuf[p][:, oc * TT:(oc + 1) * TT],
                    eexp[p][:, oc * TT:(oc + 1) * TT]))

        def emit_store(s, u, tile):
            kind = u // 4
            off = tile * TT
            outd = [qo, ko, vo][kind]
            bb = u - 4 * kind
            p = u % 2
            s.sync(lambda c, outd=outd, bb=bb, off=off, p=p: c.dma(
                outd[bb * 512:(bb + 1) * 512, off:off + TT]
                .rearrange("(oc p) n -> p oc n", p=128),
                outb[p][:].rearrange("p (oc n) -> p oc n", oc=4)))

        for tile in range(NTILES):
            off = tile * TT
            s = sp.stage(f"d_xld{tile}")
            s.sync(lambda c, off=off: c.dma(
                xt[:].rearrange("p (kc n) -> p kc n", kc=KC),
                xT[:, off:off + TT].rearrange("(kc p) n -> p kc n", p=128)))
            s.sync(lambda c, off=off: c.dma(cost[:], cosd[:, off:off + TT]))
            s.sync(lambda c, off=off: c.dma(sint[:], sind[:, off:off + TT]))
            xc = [xt[:, i * TT:(i + 1) * TT] for i in range(KC)]
            emit_ln_stats(sp, nc, t, xc, TT, "dln")
            emit_ln_apply(sp, f"d_xln{tile}", xc,
                          [xln[:, i * TT:(i + 1) * TT] for i in range(KC)],
                          t["dln_mb"], t["dln_rb"], scr, TT)
            for ob in range(12):
                kind = ob // 4
                prev_qk = ob >= 1 and (ob - 1) // 4 < 2
                s = sp.stage(f"d_s1_{tile}_{ob}")
                if tile * 12 + ob + 1 < NTILES * 12:
                    s.gpsimd(pool.load_fn(wap((ob + 1) % 12)))
                slot = pool.use()
                def mms(c, slot=slot):
                    for oc in range(4):
                        for k in range(KC):
                            c.e.matmul(
                                psM[oc][:],
                                slot[:, k * 512 + oc * 128:
                                     k * 512 + (oc + 1) * 128],
                                xln[:, k * TT:(k + 1) * TT],
                                start=(k == 0), stop=(k == KC - 1))
                s.tensor(mms)
                if prev_qk:
                    emit_rope(s, ob - 1)
                if ob >= 2 and (ob - 2) // 4 < 2:
                    emit_store(s, ob - 2, tile)
                s = sp.stage(f"d_s2_{tile}_{ob}")
                if kind == 2:
                    p = ob % 2
                    for oc in range(4):
                        s.vector(lambda c, oc=oc, p=p: c.e.tensor_copy(
                            outb[p][:, oc * TT:(oc + 1) * TT], psM[oc][:]))
                else:
                    def qeqb(c, ob=ob):
                        v = c.e
                        p = ob % 2
                        for oc in range(4):
                            v.tensor_copy(ebuf[p][:, oc * TT:(oc + 1) * TT],
                                          psM[oc][:])
                        v.drain()
                        v.tensor_copy(qkb[p][:], ebuf[p][:])
                    s.vector(qeqb)
                if prev_qk:
                    emit_elu(s, ob - 1)
                s = sp.stage(f"d_s3_{tile}_{ob}")
                if kind != 2:
                    def rmm(c, ob=ob):
                        p = ob % 2
                        for oc in range(4):
                            c.e.matmul(psR[oc][:], rott[:],
                                       qkb[p][:, oc * TT:(oc + 1) * TT],
                                       start=True, stop=True)
                    s.tensor(rmm)
                else:
                    emit_store(s, ob, tile)
                if prev_qk:
                    emit_elu2(s, ob - 1)
        assert pool.load_i == pool.use_i == NTILES * 12


def phase_e1(nc, sp, t, io, st1loc):
    T = TPC
    kd, vd = io["k1_d"], io["v1_d"]
    identd = io["ident"]
    with ExitStack() as ctx:
        sb = lambda nm, sh, dt=F32: ctx.enter_context(nc.sbuf_tensor(nm, sh, dt))
        psf = lambda nm, sh, dt=F32: ctx.enter_context(
            nc.psum_tensor(nm, sh, dt))
        kt = sb("e1_kt", [128, T], BF16)
        vt = sb("e1_vt", [128, T], BF16)
        identt = sb("e1_identt", [128, 128], BF16)
        tpb = sb("e1_tpb", [128, 8 * 128], BF16)
        ps_tp = psf("e1_ps_tp", [128, 8 * 128], BF16)
        ps_S = psf("e1_ps_S", [128, HDIM])
        s = sp.stage("e1_const")
        s.gpsimd(lambda c: c.dma(identt[:], identd))
        for h in range(NHEAD):
            s = sp.stage(f"e1_ld{h}")
            s.sync(lambda c, h=h: c.dma(kt[:], kd[h * 128:(h + 1) * 128, :]))
            s.sync(lambda c, h=h: c.dma(vt[:], vd[h * 128:(h + 1) * 128, :]))
            for grp in range(4):
                s = sp.stage(f"e1_tr{h}_{grp}")
                def tr(c, grp=grp):
                    e = c.e
                    for j in range(4):
                        c0 = (grp * 4 + j) * 128
                        e.transpose(ps_tp[:, j * 128:(j + 1) * 128],
                                    kt[:, c0:c0 + 128], identt[:])
                        e.transpose(ps_tp[:, (4 + j) * 128:(5 + j) * 128],
                                    vt[:, c0:c0 + 128], identt[:])
                s.tensor(tr)
                s = sp.stage(f"e1_ev{h}_{grp}")
                s.vector(lambda c: c.e.tensor_copy(tpb[:], ps_tp[:]))
                s = sp.stage(f"e1_mm{h}_{grp}")
                def mm(c, grp=grp):
                    for j in range(4):
                        c.e.matmul(ps_S[:], tpb[:, j * 128:(j + 1) * 128],
                                   tpb[:, (4 + j) * 128:(5 + j) * 128],
                                   start=(grp == 0 and j == 0),
                                   stop=(grp == 3 and j == 3))
                s.tensor(mm)
            s = sp.stage(f"e1_fin{h}")
            def fin(c, h=h):
                v = c.e
                v.tensor_copy(st1loc[:, h * 129:h * 129 + 128], ps_S[:])
                v.reduce_sum(st1loc[:, h * 129 + 128:h * 129 + 129], kt[:],
                             axis=AX)
            s.vector(fin)


def phase_e2(nc, sp, t, io, st1rem, selt):
    T = TPC
    NCH = T // CHUNK
    qd, kd, vd = io["q_d"], io["k1_d"], io["v1_d"]
    trild = io["trilT"]
    identd = io["ident"]
    oo = io["ho_d"]
    with ExitStack() as ctx:
        sb = lambda nm, sh, dt=F32: ctx.enter_context(nc.sbuf_tensor(nm, sh, dt))
        psf = lambda nm, sh, dt=F32: ctx.enter_context(
            nc.psum_tensor(nm, sh, dt))
        qt = [sb(f"e2_qt{i}", [128, T], BF16) for i in range(2)]
        kt = [sb(f"e2_kt{i}", [128, T], BF16) for i in range(2)]
        vt = [sb(f"e2_vt{i}", [128, T], BF16) for i in range(2)]
        trilt = sb("e2_trilt", [128, 2 * CHUNK])
        identt = sb("e2_identt", [128, 128], BF16)
        onesr = sb("e2_onesr", [1, 128], F32)
        onesb = sb("e2_onesb", [128, 1], BF16)
        attb = [sb(f"e2_attb{i}", [128, 2 * CHUNK], BF16) for i in range(2)]
        tpb = [sb(f"e2_tpb{i}", [128, 4 * 128], BF16) for i in range(2)]
        Ssb = sb("e2_Ssb", [128, HDIM])
        Sb16 = sb("e2_Sb16", [128, HDIM], BF16)
        Zsb = sb("e2_Zsb", [128, 1])
        Zb16 = sb("e2_Zb16", [128, 1], BF16)
        zscr = sb("e2_zscr", [128, 1])
        numsb = sb("e2_numsb", [128, T])
        densb = sb("e2_densb", [1, T])
        outhb = sb("e2_outhb", [128, T], BF16)
        ps_att = [psf(f"e2_ps_att{i}", [128, 2 * CHUNK]) for i in range(2)]
        ps_tp = [psf(f"e2_ps_tp{i}", [128, 4 * 128], BF16) for i in range(2)]
        ps_num = psf("e2_ps_num", [128, CHUNK])
        ps_den = psf("e2_ps_den", [1, CHUNK])
        ps_S = psf("e2_ps_S", [128, HDIM])
        ps_rec = psf("e2_ps_rec", [128, TT])
        s = sp.stage("e2_const")
        s.sync(lambda c: c.dma(trilt[:].rearrange("p (mc l) -> p mc l", mc=2),
                               trild.rearrange("(mc p) l -> p mc l", p=128)))
        s.sync(lambda c: c.dma(onesr[:], io["ones_row"]))
        s.gpsimd(lambda c: c.dma(identt[:], identd))
        s.gpsimd(lambda c: c.dma(onesb[:],
                                 io["ones_row"].rearrange("one p -> p one"),
                                 slow=True))
        s.sync(lambda c: c.dma(qt[0][:], qd[0:128, :]))
        s.sync(lambda c: c.dma(kt[0][:], kd[0:128, :]))
        s.sync(lambda c: c.dma(vt[0][:], vd[0:128, :]))

        for h in range(NHEAD):
            hp = h % 2
            qh, kh, vh = qt[hp], kt[hp], vt[hp]

            def amm_fn(ch, c, qh=qh, kh=kh, vh=vh):
                c0 = ch * CHUNK
                qc = qh[:, c0:c0 + CHUNK]
                kc = kh[:, c0:c0 + CHUNK]
                vc = vh[:, c0:c0 + CHUNK]
                e = c.e
                pa, pt_ = ps_att[ch % 2], ps_tp[ch % 2]
                for mc in range(2):
                    e.matmul(pa[:, mc * CHUNK:(mc + 1) * CHUNK],
                             kc[:, mc * 128:(mc + 1) * 128], qc,
                             start=True, stop=True)
                for mc in range(2):
                    e.transpose(pt_[:, mc * 128:(mc + 1) * 128],
                                kc[:, mc * 128:(mc + 1) * 128], identt[:])
                    e.transpose(pt_[:, (2 + mc) * 128:(3 + mc) * 128],
                                vc[:, mc * 128:(mc + 1) * 128], identt[:])

            def mask_fn(ch, c):
                v = c.e
                pa, pt_ = ps_att[ch % 2], ps_tp[ch % 2]
                for mc in range(2):
                    v.tensor_mul(attb[ch % 2][:, mc * CHUNK:(mc + 1) * CHUNK],
                                 pa[:, mc * CHUNK:(mc + 1) * CHUNK],
                                 trilt[:, mc * CHUNK:(mc + 1) * CHUNK])
                v.tensor_copy(tpb[ch % 2][:], pt_[:])

            def nmm_fn(ch, c, qh=qh):
                c0 = ch * CHUNK
                qc = qh[:, c0:c0 + CHUNK]
                e = c.e
                ab, tb = attb[ch % 2], tpb[ch % 2]
                for mc in range(2):
                    e.matmul(ps_num[:], tb[:, (2 + mc) * 128:(3 + mc) * 128],
                             ab[:, mc * CHUNK:(mc + 1) * CHUNK],
                             start=(mc == 0), stop=False)
                e.matmul(ps_num[:], Sb16[:], qc, start=False, stop=True)
                for mc in range(2):
                    e.matmul(ps_den[:], onesb[:],
                             ab[:, mc * CHUNK:(mc + 1) * CHUNK],
                             start=(mc == 0), stop=False)
                e.matmul(ps_den[:], Zb16[:], qc, start=False, stop=True)
                for mc in range(2):
                    e.matmul(ps_S[:], tb[:, mc * 128:(mc + 1) * 128],
                             tb[:, (2 + mc) * 128:(3 + mc) * 128],
                             start=(mc == 0), stop=(mc == 1))

            def upd_fn(ch, c, kh=kh):
                c0 = ch * CHUNK
                kc = kh[:, c0:c0 + CHUNK]
                v = c.e
                v.tensor_copy(numsb[:, c0:c0 + CHUNK], ps_num[:])
                v.tensor_copy(densb[0:1, c0:c0 + CHUNK], ps_den[:])
                v.reduce_sum(zscr[:], kc, axis=AX)
                v.tensor_add(Ssb[:], Ssb[:], ps_S[:])
                v.drain()
                v.tensor_add(Zsb[:], Zsb[:], zscr[:])
                v.drain()
                v.tensor_copy(Sb16[:], Ssb[:])
                v.tensor_copy(Zb16[:], Zsb[:])

            s = sp.stage(f"e2_init{h}")
            def init(c, h=h):
                v = c.e
                v.tensor_scalar_mul(Ssb[:], st1rem[:, h * 129:h * 129 + 128],
                                    selt[:])
                v.tensor_scalar_mul(Zsb[:],
                                    st1rem[:, h * 129 + 128:h * 129 + 129],
                                    selt[:])
                v.drain()
                v.tensor_copy(Sb16[:], Ssb[:])
                v.tensor_copy(Zb16[:], Zsb[:])
            s.vector(init)
            s = sp.stage(f"e2_pre{h}")
            s.tensor(lambda c, f=amm_fn: f(0, c))
            for ch in range(NCH):
                s = sp.stage(f"e2_s1_{h}_{ch}")
                if ch + 1 < NCH:
                    s.tensor(lambda c, ch=ch, f=amm_fn: f(ch + 1, c))
                s.vector(lambda c, ch=ch, f=mask_fn: f(ch, c))
                if ch >= 1:
                    s.vector(lambda c, ch=ch, f=upd_fn: f(ch - 1, c))
                s = sp.stage(f"e2_s2_{h}_{ch}")
                s.tensor(lambda c, ch=ch, f=nmm_fn: f(ch, c))
                if ch == 0 and h + 1 < NHEAD:
                    hn = h + 1
                    s.sync(lambda c, hn=hn: c.dma(
                        qt[hn % 2][:], qd[hn * 128:(hn + 1) * 128, :]))
                    s.sync(lambda c, hn=hn: c.dma(
                        kt[hn % 2][:], kd[hn * 128:(hn + 1) * 128, :]))
                    s.sync(lambda c, hn=hn: c.dma(
                        vt[hn % 2][:], vd[hn * 128:(hn + 1) * 128, :]))
            s = sp.stage(f"e2_fin{h}")
            s.vector(lambda c, f=upd_fn: f(NCH - 1, c))
            s = sp.stage(f"e2_dmax{h}")
            s.vector(lambda c: c.e.tensor_scalar_max(densb[:], densb[:], 1e-6))
            s = sp.stage(f"e2_dln{h}")
            def dln(c):
                c.e.activation(densb[:], densb[:], AF.Ln)
                c.e.drain()
                c.e.activation(densb[:], densb[:], AF.Exp, scale=-1.0)
            s.scalar(dln)
            for tb in range(T // TT):
                s = sp.stage(f"e2_rb{h}_{tb}")
                s.tensor(lambda c, tb=tb: c.e.matmul(
                    ps_rec[:], onesr[:], densb[0:1, tb * TT:(tb + 1) * TT],
                    start=True, stop=True))
                s = sp.stage(f"e2_ro{h}_{tb}")
                s.vector(lambda c, tb=tb: c.e.tensor_mul(
                    outhb[:, tb * TT:(tb + 1) * TT],
                    numsb[:, tb * TT:(tb + 1) * TT], ps_rec[:]))
            s = sp.stage(f"e2_hst{h}")
            s.sync(lambda c, h=h: c.dma(oo[h * 128:(h + 1) * 128, :],
                                        outhb[:]))


def phase_f(nc, sp, t, io):
    hoT, xT = io["ho_d"], io["x2_d"]
    woT, w0T, b0d, w1T, b1d = (io["woutT"], io["w0T"], io["b0"],
                               io["w1T"], io["b1"])
    xo = io["x_out"]
    QH = 1024
    NQ = HID // QH
    with ExitStack() as ctx:
        sb = lambda nm, sh, dt=F32: ctx.enter_context(nc.sbuf_tensor(nm, sh, dt))
        psf = lambda nm, sh, dt=F32: ctx.enter_context(
            nc.psum_tensor(nm, sh, dt))
        xt = sb("f_xt", [128, KC * TT])
        hx = sb("f_hx", [128, KC * TT], BF16)
        ao = sb("f_ao", [128, KC * TT])
        scr = sb("f_scr", [128, 4 * TT])
        act_q = sb("f_act_q", [128, 8 * TT], BF16)
        actf4 = [sb(f"f_actf4{i}", [128, 4 * TT], F16) for i in range(2)]
        facf = sb("f_facf", [128, KC * TT], F16)
        b0t = sb("f_b0t", [128, 64])
        b1t = sb("f_b1t", [128, 16])
        ps8 = [psf(f"f_b{i}", [128, TT]) for i in range(8)]
        psG = ps8[0:4]
        psU = ps8[4:8]
        alloc_ln(nc, ctx, t, "fln", TT, ps8)
        pool = WPool(nc, ctx, "f", 2)

        def tile_aps():
            aps = []
            for ob in range(4):
                aps.append((woT[:, ob * 512:(ob + 1) * 512].rearrange(
                    "(kc p) o -> p kc o", p=128), KC * 512))
            for q in range(NQ):
                for hb in range(2):
                    h0 = q * QH + hb * 512
                    aps.append((w0T[:, h0:h0 + 512].rearrange(
                        "(kc p) o -> p kc o", p=128), KC * 512))
                for db in range(4):
                    aps.append((w1T[q * QH:(q + 1) * QH,
                                    db * 512:(db + 1) * 512].rearrange(
                        "(hc p) o -> p hc o", p=128), 8 * 512))
            return aps

        all_aps = []
        for tile in range(NTILES):
            all_aps.extend(tile_aps())
        pi = [0]

        def pf(s):
            if pi[0] < len(all_aps) and pool.load_i - pool.use_i < pool.n:
                ap, ncols = all_aps[pi[0]]
                s.gpsimd(pool.load_fn(ap, ncols=ncols))
                pi[0] += 1

        def mmk(c, slot, psd, rhs, nk):
            for oc in range(4):
                for k in range(nk):
                    c.e.matmul(
                        psd[oc][:],
                        slot[:, k * 512 + oc * 128:k * 512 + (oc + 1) * 128],
                        rhs[:, k * TT:(k + 1) * TT],
                        start=(k == 0), stop=(k == nk - 1))

        s = sp.stage("f_bias_load")
        s.sync(lambda c: c.dma(b0t[:].rearrange("p (hc one) -> p hc one", one=1),
                               b0d.rearrange("(hc p) one -> p hc one", p=128),
                               slow=True))
        s.sync(lambda c: c.dma(b1t[:].rearrange("p (kc one) -> p kc one", one=1),
                               b1d.rearrange("(kc p) one -> p kc one", p=128),
                               slow=True))
        pf(s)

        for tile in range(NTILES):
            off = tile * TT
            s = sp.stage(f"f_ld{tile}")
            s.sync(lambda c, off=off: c.dma(
                hx[:].rearrange("p (kc n) -> p kc n", kc=KC),
                hoT[:, off:off + TT].rearrange("(kc p) n -> p kc n", p=128)))
            s.sync(lambda c, off=off: c.dma(
                xt[:].rearrange("p (kc n) -> p kc n", kc=KC),
                xT[:, off:off + TT].rearrange("(kc p) n -> p kc n", p=128)))
            pf(s)
            for ob in range(4):
                s = sp.stage(f"f_wo{tile}_{ob}")
                pf(s)
                slot = pool.use()
                s.tensor(lambda c, slot=slot, ob=ob: mmk(
                    c, slot, psG if ob % 2 == 0 else psU, hx, KC))
                if ob >= 1:
                    def ev(s, ob=ob):
                        psd = psG if (ob - 1) % 2 == 0 else psU
                        for oc in range(4):
                            s.vector(lambda c, ob=ob, oc=oc, psd=psd:
                                     c.e.tensor_copy(
                                ao[:, ((ob - 1) * 4 + oc) * TT:
                                   ((ob - 1) * 4 + oc + 1) * TT], psd[oc][:]))
                    ev(s)
            s = sp.stage(f"f_woev3_{tile}")
            for oc in range(4):
                s.vector(lambda c, oc=oc: c.e.tensor_copy(
                    ao[:, (3 * 4 + oc) * TT:(3 * 4 + oc + 1) * TT],
                    psU[oc][:]))
            pf(s)
            aoc = [ao[:, i * TT:(i + 1) * TT] for i in range(KC)]
            emit_ln_stats(sp, nc, t, aoc, TT, "fln")
            emit_ln_apply(sp, f"f_lna1_{tile}", aoc, aoc,
                          t["fln_mb"], t["fln_rb"], scr, TT)
            emit_res_add(sp, f"f_res1_{tile}",
                         [xt[:, i * TT:(i + 1) * TT] for i in range(KC)], aoc)
            xc = [xt[:, i * TT:(i + 1) * TT] for i in range(KC)]
            emit_ln_stats(sp, nc, t, xc, TT, "fln")
            emit_ln_apply(sp, f"f_xln{tile}", xc,
                          [hx[:, i * TT:(i + 1) * TT] for i in range(KC)],
                          t["fln_mb"], t["fln_rb"], scr, TT)
            def emit_gel(s, q, hb):
                u = q * 2 + hb
                psd = psG if u % 2 == 0 else psU
                for oc in range(4):
                    def gel(c, q=q, hb=hb, oc=oc, psd=psd):
                        hc = q * 8 + hb * 4 + oc
                        c.e.activation(actf4[hb][:, oc * TT:(oc + 1) * TT],
                                       psd[oc][:], AF.Gelu,
                                       bias=b0t[:, hc:hc + 1])
                    s.scalar(gel)

            def emit_fcb(s, hb):
                for oc in range(4):
                    s.vector(lambda c, hb=hb, oc=oc: c.e.tensor_copy(
                        act_q[:, (hb * 4 + oc) * TT:(hb * 4 + oc + 1) * TT],
                        actf4[hb][:, oc * TT:(oc + 1) * TT]))

            for q in range(NQ):
                for hb in range(2):
                    u = q * 2 + hb
                    s = sp.stage(f"f_f0_{tile}_{q}_{hb}")
                    pf(s)
                    slot = pool.use()
                    s.tensor(lambda c, slot=slot, u=u: mmk(
                        c, slot, psG if u % 2 == 0 else psU, hx, KC))
                    if hb == 1:
                        emit_gel(s, q, 0)
                s = sp.stage(f"f_gtail1_{tile}_{q}")
                pf(s)
                emit_gel(s, q, 1)
                emit_fcb(s, 0)
                s = sp.stage(f"f_gtail2_{tile}_{q}")
                pf(s)
                emit_fcb(s, 1)
                for db in range(4):
                    s = sp.stage(f"f_fdm_{tile}_{q}_{db}")
                    pf(s)
                    slot = pool.use()
                    s.tensor(lambda c, slot=slot, db=db: mmk(
                        c, slot, psG if db % 2 == 0 else psU, act_q, 8))
                    if db >= 1:
                        def fde(s, db=db, q=q):
                            psd = psG if (db - 1) % 2 == 0 else psU
                            for oc in range(4):
                                ss = slice(((db - 1) * 4 + oc) * TT,
                                           ((db - 1) * 4 + oc + 1) * TT)
                                if q == 0:
                                    s.vector(lambda c, ss=ss, oc=oc, psd=psd:
                                             c.e.tensor_copy(
                                        ao[:, ss], psd[oc][:]))
                                else:
                                    s.vector(lambda c, ss=ss, oc=oc, psd=psd:
                                             c.e.tensor_add(
                                        ao[:, ss], ao[:, ss], psd[oc][:]))
                        fde(s)
                s = sp.stage(f"f_fde3_{tile}_{q}")
                pf(s)
                for oc in range(4):
                    ss = slice((3 * 4 + oc) * TT, (3 * 4 + oc + 1) * TT)
                    if q == 0:
                        s.vector(lambda c, ss=ss, oc=oc: c.e.tensor_copy(
                            ao[:, ss], psU[oc][:]))
                    else:
                        s.vector(lambda c, ss=ss, oc=oc: c.e.tensor_add(
                            ao[:, ss], ao[:, ss], psU[oc][:]))
            s = sp.stage(f"f_fb1_{tile}")
            def b1a(c):
                v = c.e
                for r in range(4):
                    for j in range(4):
                        i = r * 4 + j
                        v.tensor_scalar_add(ao[:, i * TT:(i + 1) * TT],
                                            ao[:, i * TT:(i + 1) * TT],
                                            b1t[:, i:i + 1])
                    v.drain()
                    for j in range(4):
                        i = r * 4 + j
                        v.tensor_copy(facf[:, i * TT:(i + 1) * TT],
                                      ao[:, i * TT:(i + 1) * TT])
            s.vector(b1a)
            fc = [facf[:, i * TT:(i + 1) * TT] for i in range(KC)]
            emit_ln_stats(sp, nc, t, fc, TT, "fln")
            emit_ln_apply(sp, f"f_lna2_{tile}", fc, fc,
                          t["fln_mb"], t["fln_rb"], scr, TT)
            emit_res_add(sp, f"f_res2_{tile}",
                         [xt[:, i * TT:(i + 1) * TT] for i in range(KC)], fc)
            s = sp.stage(f"f_xst{tile}")
            s.sync(lambda c, off=off: c.dma(
                xo[:, off:off + TT].rearrange("(kc p) n -> p kc n", p=128),
                xt[:].rearrange("p (kc n) -> p kc n", kc=KC)))
            pf(s)
        assert pool.load_i == pool.use_i == len(all_aps), (
            pool.load_i, pool.use_i, len(all_aps))


def build_fused():
    nc = bass.Bass("TRN2", target_bir_lowering=False,
                   detect_race_conditions=DETECT_RACE)
    io = {}
    def ein(nm, sh, dt=F32):
        io[nm] = nc.dram_tensor(nm, sh, dt, kind="ExternalInput").ap()
    ein("xT", [D, TPC]); ein("prev", [D, 1])
    ein("wkT", [D, D]); ein("wvT", [D, D]); ein("wrT", [D, D])
    ein("tm", [D, 3]); ein("td", [D, 1]); ein("tf", [D, 1])
    ein("woT", [D, D]); ein("wgT", [D, HID]); ein("wuT", [D, HID])
    ein("wdT", [HID, D])
    ein("wqkvT", [D, 3 * D]); ein("cosd", [128, TPC]); ein("sind", [128, TPC])
    ein("rotP", [128, 128]); ein("trilT", [CHUNK, CHUNK])
    ein("ident", [128, 128])
    ein("woutT", [D, D]); ein("w0T", [D, HID]); ein("b0", [HID, 1])
    ein("w1T", [HID, D]); ein("b1", [D, 1])
    ein("ones_col", [128, 1]); ein("ones_row", [1, 128])
    ein("ctrl", [1, 8], I32); ein("sel", [128, 1])
    io["x_out"] = nc.dram_tensor("x_out", [D, TPC], F32,
                                 kind="ExternalOutput").ap()
    def internal(nm, sh, dt=F32):
        io[nm] = nc.dram_tensor(nm, sh, dt).ap()
    internal("k_d", [D, TPC]); internal("v_d", [D, TPC])
    internal("sr_d", [D, TPC]); internal("wkv_d", [D, TPC])
    internal("x2_d", [D, TPC])
    internal("q_d", [D, TPC], BF16); internal("k1_d", [D, TPC], BF16)
    internal("v1_d", [D, TPC], BF16); internal("ho_d", [D, TPC], BF16)
    sh_st0 = nc.dram_tensor("sh_st0", [256, 48], F32,
                            addr_space="Shared").ap()
    sh_st1 = nc.dram_tensor("sh_st1", [256, NHEAD * 129], F32,
                            addr_space="Shared").ap()
    sh_flag = nc.dram_tensor("sh_flag", [2, 16], I32,
                             addr_space="Shared").ap()

    with ExitStack() as octx:
        t = alloc_common(nc, octx)
        sb = lambda nm, sh, dt=F32: octx.enter_context(
            nc.sbuf_tensor(nm, sh, dt))
        ctrlt = sb("ctrlt", [1, 8], I32)
        selt = sb("selt", [128, 1])
        flagt = sb("flagt", [1, 16], I32)
        spinsem = octx.enter_context(nc.semaphore("spinsem"))
        V = {}

        sp = SP(nc)
        s = sp.stage("load_common")
        s.gpsimd(lambda c: c.dma(t["ones_col_b"][:], io["ones_col"]))
        s.sync(lambda c: c.dma(t["ones_row_f"][:], io["ones_row"]))
        s.sync(lambda c: c.dma(ctrlt[:], io["ctrl"]))
        s.sync(lambda c: c.dma(selt[:], io["sel"]))
        s = sp.stage("ctrl_vals")
        def cv(c):
            g = c.raw
            sk = dict(engines=[g.engine], skip_runtime_bounds_check=True)
            V["my"] = nc.values_load(ctrlt[0:1, 0:1], **sk)
            V["pr"] = nc.values_load(ctrlt[0:1, 1:2], **sk)
        s.gpsimd(cv)

        phase_a(nc, sp, t, io)
        with ExitStack() as bctx:
            bsb = lambda nm, sh: bctx.enter_context(
                nc.sbuf_tensor(nm, sh, F32))
            st0loc = bsb("st0loc", [128, 48])
            st0rem = bsb("st0rem", [128, 48])
            stAm = bsb("stAm", [128, 16])
            stBm = bsb("stBm", [128, 16])
            st = {"st0loc": st0loc, "stAm": stAm, "stBm": stBm,
                  "stE": st0rem[:, 32:48]}
            phase_scan(nc, sp, t, io, "b1", False, st)

            def set_nonce0(c):
                g = c.raw
                V["nonce"] = nc.values_load(
                    ctrlt[0:1, 2:3], engines=[g.engine],
                    skip_runtime_bounds_check=True)
            s = sp.stage("nonce0")
            s.gpsimd(set_nonce0)
            emit_exchange(nc, sp, "x0", V, st0loc, st0rem, sh_st0, sh_flag, 0,
                          ctrlt, flagt, spinsem)
            s = sp.stage("b2_mask")
            def msk(c):
                v = c.e
                v.tensor_scalar_mul(stAm[:], st0rem[:, 0:16], selt[:])
                v.tensor_scalar_mul(stBm[:], st0rem[:, 16:32], selt[:])
            s.vector(msk)
            phase_scan(nc, sp, t, io, "b2", True, st)
        phase_c(nc, sp, t, io)
        phase_d(nc, sp, t, io)
        with ExitStack() as ectx:
            st1loc = ectx.enter_context(
                nc.sbuf_tensor("st1loc", [128, NHEAD * 129], F32))
            st1rem = ectx.enter_context(
                nc.sbuf_tensor("st1rem", [128, NHEAD * 129], F32))
            phase_e1(nc, sp, t, io, st1loc)

            def set_nonce1(c):
                g = c.raw
                V["nonce"] = nc.values_load(
                    ctrlt[0:1, 3:4], engines=[g.engine],
                    skip_runtime_bounds_check=True)
            s = sp.stage("nonce1")
            s.gpsimd(set_nonce1)
            emit_exchange(nc, sp, "x1", V, st1loc, st1rem, sh_st1, sh_flag, 1,
                          ctrlt, flagt, spinsem)
            phase_e2(nc, sp, t, io, st1rem, selt)
        phase_f(nc, sp, t, io)
        sp.emit(octx)
    return nc


def _f32c(a):
    return np.ascontiguousarray(np.asarray(a, np.float32))


def kernel(L0_attn_x, L0_attn_time_decay, L0_attn_time_first,
           L0_attn_time_mix_k, L0_attn_time_mix_v, L0_attn_time_mix_r,
           L0_attn_Wk, L0_attn_Wv, L0_attn_Wr, L0_attn_Wo,
           L0_ffn_w_gate, L0_ffn_w_up, L0_ffn_w_down,
           L1_attn_W_qkv, L1_attn_W_out,
           L1_ffn_W0, L1_ffn_bias0, L1_ffn_W1, L1_ffn_bias1):
    B, T = 4, 4096
    x = _f32c(L0_attn_x)
    xTc = [np.ascontiguousarray(x[c // 2, (c % 2) * TPC:(c % 2 + 1) * TPC].T)
           for c in range(NCORES)]
    prevs = [np.zeros((D, 1), np.float32) if c % 2 == 0 else
             np.ascontiguousarray(x[c // 2, TPC - 1:TPC].T)
             for c in range(NCORES)]
    ones_col = np.ones((128, 1), np.float32)
    ones_row = np.ones((1, 128), np.float32)
    wkT = np.ascontiguousarray(_f32c(L0_attn_Wk).T)
    wvT = np.ascontiguousarray(_f32c(L0_attn_Wv).T)
    wrT = np.ascontiguousarray(_f32c(L0_attn_Wr).T)
    tm = np.ascontiguousarray(np.stack(
        [_f32c(L0_attn_time_mix_k).reshape(D),
         _f32c(L0_attn_time_mix_v).reshape(D),
         _f32c(L0_attn_time_mix_r).reshape(D)], axis=1))
    td = _f32c(L0_attn_time_decay).reshape(D, 1)
    tf = _f32c(L0_attn_time_first).reshape(D, 1)
    woT = np.ascontiguousarray(_f32c(L0_attn_Wo).T)
    wgT = np.ascontiguousarray(_f32c(L0_ffn_w_gate).T)
    wuT = np.ascontiguousarray(_f32c(L0_ffn_w_up).T)
    wdT = np.ascontiguousarray(_f32c(L0_ffn_w_down).T)
    wqkvT = np.ascontiguousarray(_f32c(L1_attn_W_qkv).T)
    inv = 1.0 / (10000.0 ** (np.arange(0, HDIM, 2, dtype=np.float32) / HDIM))
    pos = np.arange(T, dtype=np.float32)
    fr = np.outer(pos, inv)
    emb = np.concatenate([fr, fr], axis=-1)
    cosT = np.ascontiguousarray(np.cos(emb).T.astype(np.float32))
    sinT = np.ascontiguousarray(np.sin(emb).T.astype(np.float32))
    P = np.zeros((128, 128), np.float32)
    for i in range(64):
        P[2 * i, 2 * i + 1] = -1.0
        P[2 * i + 1, 2 * i] = 1.0
    rotP = np.ascontiguousarray(P.T)
    trilT = np.ascontiguousarray(np.triu(np.ones((CHUNK, CHUNK), np.float32)))
    ident = np.eye(128, dtype=np.float32)
    woutT = np.ascontiguousarray(_f32c(L1_attn_W_out).T)
    w0T = np.ascontiguousarray(_f32c(L1_ffn_W0).T)
    w1T = np.ascontiguousarray(_f32c(L1_ffn_W1).T)
    b0 = _f32c(L1_ffn_bias0).reshape(HID, 1)
    b1 = _f32c(L1_ffn_bias1).reshape(D, 1)
    rng = np.random.default_rng()
    nonce0 = int(rng.integers(1, 2**30))
    nonce1 = nonce0 + 1

    nc = build_fused()
    maps = []
    for c in range(NCORES):
        hf = c % 2
        sel = np.full((128, 1), float(hf), np.float32)
        ctrl = np.array([[hf, 1 - hf, nonce0, nonce1, 0, 0, 0, 0]], np.int32)
        maps.append({
            "xT": xTc[c], "prev": prevs[c],
            "wkT": wkT, "wvT": wvT, "wrT": wrT, "tm": tm,
            "td": td, "tf": tf,
            "woT": woT, "wgT": wgT, "wuT": wuT, "wdT": wdT,
            "wqkvT": wqkvT,
            "cosd": np.ascontiguousarray(cosT[:, hf * TPC:(hf + 1) * TPC]),
            "sind": np.ascontiguousarray(sinT[:, hf * TPC:(hf + 1) * TPC]),
            "rotP": rotP, "trilT": trilT, "ident": ident,
            "woutT": woutT, "w0T": w0T, "b0": b0, "w1T": w1T, "b1": b1,
            "ones_col": ones_col, "ones_row": ones_row,
            "ctrl": ctrl, "sel": sel,
        })
    res = run_bass_kernel_spmd(nc, maps, core_ids=list(range(NCORES)))
    out = np.zeros((B, T, D), np.float32)
    for c in range(NCORES):
        b, hf = c // 2, c % 2
        out[b, hf * TPC:(hf + 1) * TPC] = np.asarray(
            res.results[c]["x_out"]).T
    return out
